# version 18
# baseline (speedup 1.0000x reference)
import numpy as np
import ml_dtypes
from contextlib import ExitStack

import concourse.bass as bass
import concourse.mybir as mybir
from concourse.bass_utils import run_bass_kernel_spmd

F32 = mybir.dt.float32
BF16 = mybir.dt.bfloat16
F16 = mybir.dt.float16
AF = mybir.ActivationFunctionType
ALU = mybir.AluOpType
AX = mybir.AxisListType

D, NH, HD, LH, LD, F = 1024, 16, 64, 8, 512, 4096
NEG = -65536.0
EPS = 1e-5

def _pvoff(h):
    return 512 * (h // 7) + 65 * (h % 7)

class Tok:
    __slots__ = ("sem", "val")
    def __init__(self, sem, val): self.sem, self.val = sem, val

def build_program(S=8192):
    NG = S // 512
    NQ = S // 128
    nc = bass.Bass("TRN2", target_bir_lowering=False, debug=False)
    cnt = {}
    def mk_inc(sem):
        def inc(instr, n=1):
            cnt[sem.name] = cnt.get(sem.name, 0) + n
            instr.then_inc(sem, n)
            return Tok(sem, cnt[sem.name])
        return inc
    def wait(eng, *toks):
        best = {}
        for t in toks:
            if t is None: continue
            k = t.sem.name
            if k not in best or best[k].val < t.val: best[k] = t
        for t in best.values(): eng.wait_ge(t.sem, t.val)

    x_d = nc.dram_tensor("x", [S, D], F32, kind="ExternalInput").ap()
    out_d = nc.dram_tensor("out", [S, D], F32, kind="ExternalOutput").ap()
    x2_d = nc.dram_tensor("x2s", [S, D], F32, kind="Internal").ap()
    din = {}
    for nm, shp in [("wq",[D,D]),("wk",[D,D]),("wv",[D,D]),("wgq",[D,LD]),("wgk",[D,LD]),
                    ("wgv",[D,LD]),("rout",[LD,D]),("wout",[D,D]),("gblk",[D,32]),
                    ("wit",[D,F]),("wot",[F,D]),("glA",[D,40]),("glb",[1,40]),
                    ("lorB",[32,F]),("wib",[128,32]),("wob",[1,D])]:
        din[nm] = nc.dram_tensor(nm, shp, F32, kind="ExternalInput").ap()
    witb_d = nc.dram_tensor("witb", [128, 8*F], BF16, kind="Internal").ap()
    wotb_d = nc.dram_tensor("wotb", [128, 32*D], BF16, kind="Internal").ap()
    ident_d = nc.dram_tensor("identc", [128,128], BF16, kind="ExternalInput").ap()
    mask_d = nc.dram_tensor("maskc", [128,384], F16, kind="ExternalInput").ap()
    onesb_d = nc.dram_tensor("onesb", [1,128], BF16, kind="ExternalInput").ap()

    ctx = ExitStack()
    ctxA = ExitStack()
    def sb(stk, name, shape, dtype=BF16): return stk.enter_context(nc.sbuf_tensor(name, shape, dtype))
    def ps(name, shape, dtype=F32): return ctx.enter_context(nc.psum_tensor(name, shape, dtype))

    ident = sb(ctx,"ident_s",[128,128]); maskc = sb(ctx,"maskc_s",[128,384],F16)
    onesb = sb(ctx,"onesb_s",[1,128])
    stats = sb(ctx,"stats_s",[128,32],F32)
    stage = sb(ctx,"stage",[128,2048],F32)

    xbuf = sb(ctxA,"xbuf",[128,4,D],F32)
    x2buf = sb(ctxA,"x2buf",[128,2,D],F32)
    wq = sb(ctxA,"wq_s",[128,8,D]); wk = sb(ctxA,"wk_s",[128,8,D]); wv = sb(ctxA,"wv_s",[128,8,D])
    wout = sb(ctxA,"wout_s",[128,8,D]); wgq = sb(ctxA,"wgq_s",[128,8,LD])
    wgk = sb(ctxA,"wgk_s",[128,8,LD]); wgv = sb(ctxA,"wgv_s",[128,8,LD])
    rout = sb(ctxA,"rout_s",[128,4,D]); gblk = sb(ctxA,"gblk_s",[128,8,32])
    ln1 = sb(ctxA,"ln1",[128,4,D])
    ln1T = sb(ctxA,"ln1T",[128,8,512])
    QT = sb(ctxA,"QT",[128,8,512])
    KT = sb(ctxA,"KT",[128,8,768])
    VV = sb(ctxA,"VV",[128,6,16*65],F16)
    VVv = VV[:].rearrange("p s (h c) -> p s h c", h=16)
    GQT = sb(ctxA,"GQT",[128,4,512])
    PL = sb(ctxA,"PL",[128,4,384],F16)
    PRL = sb(ctxA,"PRL",[64,8,128],F16)
    loc = sb(ctxA,"loc",[128,2,D])
    attn = ln1
    attnT = sb(ctxA,"attnT",[128,8,256])
    remp = sb(ctxA,"remp",[128,LD]); rempT = sb(ctxA,"rempT",[128,4,128])
    pooledT = sb(ctxA,"pooledT",[128,8,64])
    GKT = sb(ctxA,"GKT",[128,4,64]); GVA = sb(ctxA,"GVA",[64,8,65],F16); GVs = sb(ctxA,"GVs",[4,LD])
    gq_g = sb(ctxA,"gq_g",[128,4,32],F32)
    sig2 = sb(ctxA,"sig2",[128,32,4],F32)
    aL2 = sb(ctxA,"aL2",[128,16,4],F32)
    aR2 = sb(ctxA,"aR2",[128,8,4],F32)

    T = ps("T",[128,1024],BF16)
    SC = [ps(f"SC{i}",[128,512],F32) for i in range(2)]
    PVL = ps("PVL",[128,1536],F32)
    _pvv = PVL[:].rearrange("p (b c) -> p b c", b=3)
    pv_den = [_pvv[:, 0, 0:455].rearrange("p (h c) -> p h c", h=7)[:, :, 64:65],
              _pvv[:, 1, 0:455].rearrange("p (h c) -> p h c", h=7)[:, :, 64:65],
              _pvv[:, 2, 0:130].rearrange("p (h c) -> p h c", h=2)[:, :, 64:65]]
    WB = [ps(f"WB{i}",[128,512],F32) for i in range(2)]

    s_pe = ctx.enter_context(nc.semaphore("s_pe"))
    s_dve = ctx.enter_context(nc.semaphore("s_dve"))
    s_act = ctx.enter_context(nc.semaphore("s_act"))
    s_pool = ctx.enter_context(nc.semaphore("s_pool"))
    s_const = ctx.enter_context(nc.semaphore("s_const"))
    s_xld = [ctx.enter_context(nc.semaphore(f"s_xld{i}")) for i in range(4)]
    s_x2w = [ctx.enter_context(nc.semaphore(f"s_x2w{i}")) for i in range(2)]
    s_wc = [ctx.enter_context(nc.semaphore(f"s_wc{i}")) for i in range(4)]
    s_wib = ctx.enter_context(nc.semaphore("s_wib"))
    s_x2r = [ctx.enter_context(nc.semaphore(f"s_x2r{i}")) for i in range(2)]
    s_outw = [ctx.enter_context(nc.semaphore(f"s_outw{i}")) for i in range(2)]
    s_gv = ctx.enter_context(nc.semaphore("s_gv"))
    s_pci = ctx.enter_context(nc.semaphore("s_pci"))
    s_pco = ctx.enter_context(nc.semaphore("s_pco"))
    ipe, idve, iact, ipool = (mk_inc(s) for s in (s_pe,s_dve,s_act,s_pool))
    _dma_inc = {}
    def dinc(sem):
        if sem.name not in _dma_inc: _dma_inc[sem.name] = mk_inc(sem)
        return lambda instr: _dma_inc[sem.name](instr, 16)

    blk = ctx.enter_context(nc.Block())
    def pe(f): blk.tensor(f)
    def ve(f): blk.vector(f)
    def ac(f): blk.scalar(f)
    def po(f): blk.gpsimd(f)
    def sy(f): blk.sync(f)

    t_x = [None]*(NQ+8)
    t_xfree = [None]*(NQ+8)
    t_x2st = [None]*NQ
    TB_w = [None]; TB_r = [None]
    WB_w = [None, None]; WB_r = [None, None]

    pro = {}
    def _(sync):
        dinc(s_const)(sync.dma_start(out=ident[:], in_=ident_d))
        dinc(s_const)(sync.dma_start(out=maskc[:], in_=mask_d))
        pro['c2'] = dinc(s_const)(sync.dma_start(out=onesb[:], in_=onesb_d))
        for gq in range(min(4, NQ)):
            t_x[gq] = dinc(s_xld[gq % 4])(sync.dma_start(out=xbuf[:, gq % 4, :], in_=x_d[gq*128:(gq+1)*128, :]))
    sy(_)
    vones = {}
    def _(vector):
        vector.memset(VVv[:, :, :, 64:65], 1.0)
        vones['v'] = idve(vector.memset(GVA[:, :, 64:65], 1.0))
    ve(_)
    t_vones = vones['v']
    def cast_weights(plan, stg):
        nslot = stg.shape[1] // 1024
        slots = [None]*nslot
        last = None
        for i, (src, dstv, sc, P) in enumerate(plan):
            h = i % nslot
            prev = slots[h]
            tk = {}
            n = src.shape[-1]
            def _(eng, src=src, h=h, prev=prev, n=n, P=P, tk=tk):
                if prev is not None: wait(eng, prev)
                tk['ld'] = dinc(s_wc[h])(eng.dma_start(out=stg[0:P, h*1024:h*1024+n], in_=src))
            (sy if (nslot == 2 or i % 2 == 0) else po)(_)
            def _(vector, tk=tk, dstv=dstv, sc=sc, n=n, h=h, P=P):
                wait(vector, tk['ld'])
                srcv = stg[0:P, h*1024:h*1024+n]
                if sc is None:
                    tk['c'] = idve(vector.tensor_copy(out=dstv, in_=srcv))
                else:
                    tk['c'] = idve(vector.tensor_scalar(out=dstv, in0=srcv, scalar1=float(sc), scalar2=None, op0=ALU.mult))
            ve(_)
            slots[h] = tk['c']
            last = tk['c']
        return last
    def wplan(names_dsts):
        plan = []
        for nm, dst, sc in names_dsts:
            d = din[nm]; rows, cols = d.shape
            dstv = dst[:].rearrange("p a b -> p (a b)") if len(dst.shape) == 3 else dst[:]
            if rows >= 128:
                view = d.rearrange("(c p) n -> p c n", p=128)
                nch = rows // 128
                for c in range(nch):
                    for off in range(0, cols, 1024):
                        n = min(1024, cols - off)
                        plan.append((view[:, c, off:off+n], dstv[:, c*cols+off:c*cols+off+n], sc, 128))
            else:
                for off in range(0, cols, 1024):
                    n = min(1024, cols - off)
                    plan.append((d[0:rows, off:off+n], dstv[0:rows, off:off+n], sc, rows))
        return plan
    t_wA = cast_weights(wplan([("wq",wq,None),("wk",wk,0.125),("wv",wv,None),("wout",wout,None),
                               ("wgq",wgq,None),("wgk",wgk,1.0/1024),("wgv",wgv,1.0/128),
                               ("rout",rout,None),("gblk",gblk,None)]), stage)
    t_const = pro['c2']

    def transpose_tiles(srcs, dst_evac, pre_wait=None, rows=128):
        res = {}
        def _(tensor):
            wait(tensor, TB_r[0], pre_wait, t_const)
            ncols = 0; last = None
            for s in srcs:
                w0 = s.shape[1]
                last = tensor.transpose(T[0:w0, ncols:ncols+128], s, ident[:])
                ncols += 128
            res['w'] = ipe(last); TB_w[0] = res['w']
        pe(_)
        def _(scalar):
            wait(scalar, res['w'])
            ncols = 128*len(srcs)
            srcap = T[0:rows, 0:ncols]
            if len(dst_evac.shape) == 3:
                srcap = srcap.rearrange("p (a b) -> p a b", a=dst_evac.shape[1])
            TB_r[0] = iact(scalar.copy(out=dst_evac, in_=srcap))
            res['r'] = TB_r[0]
        ac(_)
        return res['w'], res['r']

    def layer_norm(nqs, srcs, src_toks, dsts, scratch, pre=()):
        toks = {}
        def _(vector):
            wait(vector, *pre, *src_toks)
            for qs in range(nqs):
                ii = vector.reduce_sum(out=stats[:, qs:qs+1], in_=srcs[qs], axis=AX.X)
            toks['s1'] = idve(ii)
        ve(_)
        def _(scalar):
            wait(scalar, *pre)
            for qs in range(nqs):
                wait(scalar, src_toks[qs])
                scalar.activation(out=scratch, in_=srcs[qs], func=AF.Square, accum_out=stats[:, 4+qs:5+qs])
            wait(scalar, toks['s1'])
            scalar.activation(out=stats[:, 8:8+nqs], in_=stats[:, 0:nqs], func=AF.Copy, scale=-1.0/D)
            toks['nm'] = iact(scalar.activation(out=stats[:, 12:12+nqs], in_=stats[:, 0:nqs], func=AF.Square, scale=1.0/D))
        ac(_)
        def _(vector):
            wait(vector, toks['nm'])
            t = idve(vector.tensor_scalar(out=stats[:, 16:16+nqs], in0=stats[:, 4:4+nqs], scalar1=1.0/D, scalar2=EPS, op0=ALU.mult, op1=ALU.add))
            vector.wait_ge(t.sem, t.val)
            toks['var'] = idve(vector.tensor_tensor(out=stats[:, 16:16+nqs], in0=stats[:, 16:16+nqs], in1=stats[:, 12:12+nqs], op=ALU.subtract))
        ve(_)
        def _(scalar):
            wait(scalar, toks['var'])
            t = iact(scalar.activation(out=stats[:, 20:20+nqs], in_=stats[:, 16:16+nqs], func=AF.Ln))
            scalar.wait_ge(t.sem, t.val)
            toks['rstd'] = iact(scalar.activation(out=stats[:, 24:24+nqs], in_=stats[:, 20:20+nqs], func=AF.Exp, scale=-0.5))
        ac(_)
        out = []
        def _(vector):
            wait(vector, toks['rstd'])
            for qs in range(nqs):
                out.append(idve(vector.tensor_scalar(out=dsts[qs], in0=srcs[qs],
                    scalar1=stats[:, 8+qs:9+qs], scalar2=stats[:, 24+qs:25+qs],
                    op0=ALU.add, op1=ALU.mult)))
        ve(_)
        return out

    pc_pieces = []
    _vw = din["wit"].rearrange("(c p) n -> p c n", p=128)
    for c in range(8):
        for off in range(0, F, 1024):
            pc_pieces.append((_vw[:, c, off:off+1024], witb_d[:, c*F+off:c*F+off+1024]))
    _vo = din["wot"].rearrange("(c p) n -> p c n", p=128)
    for c in range(32):
        pc_pieces.append((_vo[:, c, :], wotb_d[:, c*D:(c+1)*D]))
    pc_state = {'in': None, 'cast': None, 'out': None}

    t_prev_ln_done = None
    t_attn_done = None
    t_pool = [None]*NG
    t_GKGV = [None]*NG
    t_GKT_ev = [None]*NG
    t_QTev = [None]*NG
    t_KTev = [None]*NG
    t_Vev = [None]*NQ
    t_GQTev = [None]*NG
    t_scdone = [None]*(NQ+8)
    t_rscdone = [None]*(NQ+8)
    t_pvdone = [None]*(NQ+8)
    t_sigv = [None]*NG
    sc_hist = [None, None]
    pl_hist = [None]*4
    scnt = [0]
    prl_hist = [None]*8
    t_PVLfree = None
    t_combB_prev = [None]

    for g in range(NG):
        gq0 = 4*g
        srcs = [xbuf[:, (gq0+qs) % 4, :] for qs in range(4)]
        stoks = [t_x[gq0+qs] for qs in range(4)]
        ln_toks = layer_norm(4, srcs, stoks, [ln1[:, qs, :] for qs in range(4)], stage[:, 0:1024],
                             pre=(t_prev_ln_done, t_attn_done, t_wA))
        lnT_evs = []
        for qs in range(4):
            w_, r_ = transpose_tiles([ln1[:, qs, dc*128:(dc+1)*128] for dc in range(8)],
                                     ln1T[:, :, qs*128:(qs+1)*128],
                                     pre_wait=ln_toks[qs])
            lnT_evs.append(r_)
            if qs == 3: t_prev_ln_done = w_
        t_lnT = lnT_evs[-1]
        def _(vector, g=g, evs=tuple(lnT_evs)):
            wait(vector, *evs)
            with nc.allow_low_precision(reason="pooled sums feed bf16 matmuls"):
                t_pool[g] = idve(vector.tensor_reduce(
                    out=pooledT[:, :, 4*g:4*g+4],
                    in_=ln1T[:].rearrange("p a (c t) -> p a c t", c=4), axis=AX.X, op=ALU.add))
        ve(_)
        t_qfree = t_scdone[gq0-1] if g > 0 else None
        t_gqfree = t_rscdone[gq0-1] if g > 0 else None
        qk_ev = {}
        for proj, wsb in (("q", wq), ("k", wk)):
            for mt in range(8):
                bidx = mt % 2
                res = {}
                def _(tensor, wsb=wsb, mt=mt, bidx=bidx, res=res):
                    wait(tensor, WB_r[bidx], t_lnT, t_wA)
                    last = None
                    for kc in range(8):
                        last = tensor.matmul(WB[bidx][:], wsb[:, kc, mt*128:(mt+1)*128],
                                             ln1T[:, kc, :], start=(kc == 0), stop=(kc == 7))
                    res['w'] = ipe(last); WB_w[bidx] = res['w']
                pe(_)
                if proj == "q":
                    def _(vector, mt=mt, bidx=bidx, res=res, t_qfree=t_qfree):
                        wait(vector, res['w'], t_qfree)
                        WB_r[bidx] = idve(vector.tensor_copy(out=QT[:, mt, :], in_=WB[bidx][:]))
                        qk_ev[('q', mt)] = WB_r[bidx]
                    ve(_)
                else:
                    def _(scalar, mt=mt, bidx=bidx, res=res, t_qfree=t_qfree, gq0=gq0):
                        wait(scalar, res['w'], t_qfree)
                        s0 = gq0 % 6
                        if s0 <= 2:
                            ii = scalar.copy(out=KT[:, mt, s0*128:(s0+4)*128], in_=WB[bidx][:])
                        else:
                            w1 = (6 - s0)*128
                            scalar.copy(out=KT[:, mt, s0*128:768], in_=WB[bidx][:, 0:w1])
                            ii = scalar.copy(out=KT[:, mt, 0:512-w1], in_=WB[bidx][:, w1:512])
                        WB_r[bidx] = iact(ii)
                        qk_ev[('k', mt)] = WB_r[bidx]
                    ac(_)
        t_QTev[g] = qk_ev[('q', 7)]
        t_KTev[g] = qk_ev[('k', 7)]
        for qs in range(4):
            gq = gq0 + qs
            for nh in range(2):
                bidx = nh
                res = {}
                def _(tensor, qs=qs, nh=nh, bidx=bidx, res=res):
                    wait(tensor, WB_r[bidx], t_lnT, t_wA)
                    last = None
                    for kc in range(8):
                        last = tensor.matmul(WB[bidx][:], ln1T[:, kc, qs*128:(qs+1)*128],
                                             wv[:, kc, nh*512:(nh+1)*512], start=(kc == 0), stop=(kc == 7))
                    res['w'] = ipe(last); WB_w[bidx] = res['w']
                pe(_)
                def _(vector, gq=gq, nh=nh, bidx=bidx, res=res):
                    wait(vector, res['w'], t_pvdone[gq-4] if gq >= 4 else None)
                    WB_r[bidx] = idve(vector.tensor_copy(
                        out=VVv[:, gq % 6, 8*nh:8*nh+8, 0:64],
                        in_=WB[bidx][:].rearrange("p (h d) -> p h d", h=8)))
                    res['ev'] = WB_r[bidx]
                ve(_)
                if nh == 1: t_Vev[gq] = res['ev']
        for mt in range(4):
            bidx = mt % 2
            res = {}
            def _(tensor, mt=mt, bidx=bidx, res=res):
                wait(tensor, WB_r[bidx], t_lnT, t_wA)
                last = None
                for kc in range(8):
                    last = tensor.matmul(WB[bidx][:], wgq[:, kc, mt*128:(mt+1)*128],
                                         ln1T[:, kc, :], start=(kc == 0), stop=(kc == 7))
                res['w'] = ipe(last); WB_w[bidx] = res['w']
            pe(_)
            def _(scalar, mt=mt, bidx=bidx, res=res, t_gqfree=t_gqfree):
                wait(scalar, res['w'], t_gqfree)
                WB_r[bidx] = iact(scalar.copy(out=GQT[:, mt, :], in_=WB[bidx][:]))
                res['ev'] = WB_r[bidx]
            ac(_)
            if mt == 3: t_GQTev[g] = res['ev']
        resG = {}
        def _(tensor, resG=resG, g=g):
            wait(tensor, WB_r[0], t_QTev[g], t_wA)
            last = None
            for qs in range(4):
                for kc in range(8):
                    last = tensor.matmul(WB[0][:, qs*128:qs*128+32], QT[:, kc, qs*128:(qs+1)*128],
                                         gblk[:, kc, :], start=(kc == 0), stop=(kc == 7))
            resG['w'] = ipe(last); WB_w[0] = resG['w']
        pe(_)
        def _(vector, resG=resG):
            wait(vector, resG['w'])
            WB_r[0] = idve(vector.tensor_copy(
                out=gq_g[:],
                in_=WB[0][:].rearrange("p (a b) -> p a b", a=4)[:, :, 0:32]))
            resG['ev'] = WB_r[0]
        ve(_)
        def _(scalar, resG=resG):
            wait(scalar, resG['ev'])
            resG['e'] = iact(scalar.activation(out=sig2[:].transpose([0, 2, 1]),
                                               in_=gq_g[:], func=AF.Exp, scale=-1.0))
        ac(_)
        def _(vector, resG=resG, g=g):
            wait(vector, resG['e'])
            t = idve(vector.tensor_scalar(out=sig2[:], in0=sig2[:],
                                 scalar1=1.0, scalar2=None, op0=ALU.add))
            vector.wait_ge(t.sem, t.val)
            t_sigv[g] = idve(vector.reciprocal(out=sig2[:], in_=sig2[:]))
        ve(_)
        resK = {}
        def _(tensor, g=g, resK=resK):
            wait(tensor, t_pool[g], resK_prev['ev'] if g > 0 else None, t_wA)
            last = None
            for mt in range(4):
                for kc in range(8):
                    last = tensor.matmul(SC[1][:, 384+mt*4:384+mt*4+4], wgk[:, kc, mt*128:(mt+1)*128],
                                         pooledT[:, kc, 4*g:4*g+4], start=(kc == 0), stop=(kc == 7))
            resK['w'] = ipe(last)
        pe(_)
        def _(vector, g=g, resK=resK):
            wait(vector, resK['w'])
            for mt in range(4):
                ii = vector.tensor_copy(out=GKT[:, mt, 4*g:4*g+4], in_=SC[1][:, 384+mt*4:384+mt*4+4])
            resK['ev'] = idve(ii)
            t_GKT_ev[g] = resK['ev']
        ve(_)
        resK_prev = resK
        resV = {}
        def _(tensor, g=g, resV=resV):
            wait(tensor, t_pool[g], WB_r[1], t_wA)
            last = None
            for kc in range(8):
                last = tensor.matmul(WB[1][0:4, :], pooledT[:, kc, 4*g:4*g+4],
                                     wgv[:, kc, :], start=(kc == 0), stop=(kc == 7))
            resV['w'] = ipe(last); WB_w[1] = resV['w']
        pe(_)
        def _(vector, g=g, resV=resV):
            wait(vector, resV['w'], t_GKGV[g-1] if g > 0 else None)
            WB_r[1] = idve(vector.tensor_copy(out=GVs[:], in_=WB[1][0:4, :]))
            resV['ev'] = WB_r[1]
        ve(_)
        def _(gpsimd, g=g, resV=resV):
            wait(gpsimd, resV['ev'])
            t_GKGV[g] = dinc(s_gv)(gpsimd.dma_start(
                out=GVA[4*g:4*g+4, :, 0:64],
                in_=GVs[:].rearrange("p (h d) -> p h d", h=8)))
        po(_)

        for pk in range(4*g, min(4*g+4, 0)):
            srcp, dstp = pc_pieces[pk]
            def _(sync, srcp=srcp):
                wait(sync, pc_state['cast'] if pc_state['cast'] is not None else t_wA)
                pc_state['in'] = dinc(s_pci)(sync.dma_start(out=stage[:, 1024:2048], in_=srcp))
            sy(_)
            def _(vector):
                wait(vector, pc_state['in'], pc_state['out'], t_wA)
                pc_state['cast'] = idve(vector.tensor_copy(out=stg16[:], in_=stage[:, 1024:2048]))
            ve(_)
            def _(sync, dstp=dstp):
                wait(sync, pc_state['cast'])
                pc_state['out'] = dinc(s_pco)(sync.dma_start(out=dstp, in_=stg16[:]))
            sy(_)

        for qs in range(4):
            gq = gq0 + qs
            kts = [k for k in (gq-2, gq-1, gq) if k >= 0]
            nk = len(kts)
            pv_toks = {}
            exp_toks = {}
            def bank_of(k):
                return SC[k] if k < 2 else WB[k-2]
            def emit_score(h, gq=gq, qs=qs, kts=tuple(kts), nk=nk, g=g):
                sc = scnt[0] % 4; scnt[0] += 1
                rb = (h % 2)*64; hp = h // 2
                res = {'bk': sc}
                def _(tensor, h=h, sc=sc, rb=rb, hp=hp, res=res, kts=kts, nk=nk, qs=qs, g=g):
                    wait(tensor, sc_hist[sc] if sc < 2 else WB_r[sc-2], t_QTev[g], t_KTev[g],
                         t_KTev[g-1] if (g > 0 and kts[0] < 4*g) else None, t_const)
                    bk = bank_of(sc)
                    last = None
                    for ci, kq in enumerate(kts):
                        last = tensor.matmul(bk[:, ci*128:(ci+1)*128],
                            KT[rb:rb+64, hp, (kq % 6)*128:(kq % 6)*128+128],
                            QT[rb:rb+64, hp, qs*128:(qs+1)*128],
                            start=(ci == 0), stop=(ci == nk-1),
                            tile_position=(rb, 0))
                    res['w'] = ipe(last)
                pe(_)
                return res
            def emit_exp(h, nk=nk):
                sc = sc_toks[h]['bk']; sl = h % 4
                res = {}
                def _(scalar, h=h, sc=sc, sl=sl, res=res, nk=nk):
                    wait(scalar, sc_toks[h]['w'], pl_hist[sl])
                    bk = bank_of(sc)
                    res['e'] = iact(scalar.activation(out=PL[:, sl, 0:nk*128], in_=bk[:, 0:nk*128], func=AF.Exp))
                    if sc < 2: sc_hist[sc] = res['e']
                    else: WB_r[sc-2] = res['e']
                ac(_)
                return res
            def emit_msk(h, nk=nk):
                sl = h % 4
                res = {}
                def _(vector, h=h, sl=sl, res=res, nk=nk):
                    wait(vector, exp_toks[h]['e'], t_const)
                    res['m'] = idve(vector.tensor_tensor(out=PL[:, sl, 0:nk*128], in0=PL[:, sl, 0:nk*128],
                                                         in1=maskc[:, (3-nk)*128:384], op=ALU.mult))
                ve(_)
                return res
            def emit_pv(h, gq=gq, kts=tuple(kts), nk=nk):
                sl = h % 4
                res = {}
                def _(tensor, h=h, sl=sl, res=res, kts=kts, nk=nk, gq=gq):
                    need = [msk_toks[h]['m'], t_vones]
                    for kq in kts: need.append(t_Vev[kq])
                    if h == 0 or h >= 14: need.append(t_PVLfree)
                    elif h == 1: need.append(t_combB_prev[0])
                    wait(tensor, *need)
                    off = _pvoff(h)
                    last = None
                    for ci, kq in enumerate(kts):
                        last = tensor.matmul(PVL[:, off:off+65],
                            PL[:, sl, ci*128:(ci+1)*128],
                            VVv[:, kq % 6, h, :],
                            start=(ci == 0), stop=(ci == nk-1))
                    res['w'] = ipe(last)
                    pl_hist[sl] = res['w']
                pe(_)
                return res
            sc_toks = {}; msk_toks = {}
            for h in range(NH):
                sc_toks[h] = emit_score(h)
                exp_toks[h] = emit_exp(h)
                msk_toks[h] = emit_msk(h)
                if h >= 2:
                    pv_toks[h-2] = emit_pv(h-2)
            pv_toks[NH-2] = emit_pv(NH-2)
            pv_toks[NH-1] = emit_pv(NH-1)
            t_scdone[gq] = sc_toks[NH-1]['w']
            t_pvdone[gq] = pv_toks[NH-1]['w']
            resL = {}
            def _(vector, qs=qs, g=g, resL=resL, pv_toks=pv_toks):
                wait(vector, pv_toks[14]['w'], pv_toks[15]['w'], t_sigv[g])
                t = idve(vector.reciprocal(out=aL2[:, 14:16, qs:qs+1], in_=pv_den[2]))
                vector.wait_ge(t.sem, t.val)
                t = idve(vector.tensor_tensor(out=aL2[:, 14:16, qs:qs+1], in0=aL2[:, 14:16, qs:qs+1],
                                              in1=sig2[:, 14:16, qs:qs+1], op=ALU.mult))
                vector.wait_ge(t.sem, t.val)
                for h in (14, 15):
                    ii = vector.tensor_scalar(out=loc[:, qs % 2, h*64:(h+1)*64],
                        in0=PVL[:, _pvoff(h):_pvoff(h)+64],
                        scalar1=aL2[:, h, qs:qs+1], scalar2=None, op0=ALU.mult)
                resL['b2'] = idve(ii)
            ve(_)
            def _(vector, qs=qs, resL=resL, pv_toks=pv_toks):
                wait(vector, *[pv_toks[h]['w'] for h in range(14)])
                vector.reciprocal(out=aL2[:, 0:7, qs:qs+1], in_=pv_den[0])
                t = idve(vector.reciprocal(out=aL2[:, 7:14, qs:qs+1], in_=pv_den[1]))
                vector.wait_ge(t.sem, t.val)
                t = idve(vector.tensor_tensor(out=aL2[:, 0:14, qs:qs+1], in0=aL2[:, 0:14, qs:qs+1],
                                              in1=sig2[:, 0:14, qs:qs+1], op=ALU.mult))
                vector.wait_ge(t.sem, t.val)
                for h in range(14):
                    ii = vector.tensor_scalar(out=loc[:, qs % 2, h*64:(h+1)*64],
                        in0=PVL[:, _pvoff(h):_pvoff(h)+64],
                        scalar1=aL2[:, h, qs:qs+1], scalar2=None, op0=ALU.mult)
                resL['ev'] = idve(ii)
            ve(_)
            t_combA = resL['b2']
            t_locev = resL['ev']
            nl = min(gq + 1, 64)
            rexp_toks = {}
            def emit_rsc(h, qs=qs, g=g, nl=nl):
                sc = scnt[0] % 4; scnt[0] += 1
                rb = (h % 2)*64; hp = h // 2
                res = {'bk': sc}
                def _(tensor, h=h, sc=sc, rb=rb, hp=hp, res=res, qs=qs, g=g, nl=nl):
                    wait(tensor, sc_hist[sc] if sc < 2 else WB_r[sc-2], t_GQTev[g], t_GKT_ev[g])
                    bk = bank_of(sc)
                    res['w'] = ipe(tensor.matmul(bk[0:nl, 0:128],
                        GKT[rb:rb+64, hp, 0:nl],
                        GQT[rb:rb+64, hp, qs*128:(qs+1)*128],
                        start=True, stop=True, tile_position=(rb, 0)))
                pe(_)
                return res
            def emit_rexp(h, nl=nl, gq=gq):
                sc = rsc_toks[h]['bk']
                res = {}
                def _(scalar, h=h, sc=sc, res=res, nl=nl):
                    wait(scalar, rsc_toks[h]['w'], prl_hist[h])
                    bk = bank_of(sc)
                    res['e'] = iact(scalar.activation(out=PRL[0:nl, h, :], in_=bk[0:nl, 0:128], func=AF.Exp))
                    if sc < 2: sc_hist[sc] = res['e']
                    else: WB_r[sc-2] = res['e']
                ac(_)
                return res
            def emit_rpv(h, g=g, nl=nl):
                res = {}
                def _(tensor, h=h, res=res, g=g, nl=nl):
                    wait(tensor, rexp_toks[h]['e'], t_GKGV[g], t_vones,
                         t_combA if h < 7 else t_locev)
                    off = 1024 + 65*h if h < 7 else 0
                    res['w'] = ipe(tensor.matmul(PVL[:, off:off+65],
                        PRL[0:nl, h, :], GVA[0:nl, h, :], start=True, stop=True))
                    prl_hist[h] = res['w']
                pe(_)
                return res
            rsc_toks = {}; rpv_toks = {}
            for h in range(LH):
                rsc_toks[h] = emit_rsc(h)
                rexp_toks[h] = emit_rexp(h)
                if h >= 2:
                    rpv_toks[h-2] = emit_rpv(h-2)
            rpv_toks[LH-2] = emit_rpv(LH-2)
            rpv_toks[LH-1] = emit_rpv(LH-1)
            t_rscdone[gq] = rsc_toks[LH-1]['w']
            resR = {}
            def _(vector, qs=qs, resR=resR, rpv_toks=rpv_toks):
                wait(vector, *[rpv_toks[h]['w'] for h in range(LH)])
                vector.reciprocal(out=aR2[:, 0:7, qs:qs+1],
                                  in_=_pvv[:, 2, 0:455].rearrange("p (h c) -> p h c", h=7)[:, :, 64:65])
                t = idve(vector.reciprocal(out=aR2[:, 7:8, qs:qs+1], in_=pv_den[0][:, 0:1, :]))
                vector.wait_ge(t.sem, t.val)
                for h in range(LH):
                    off = 1024 + 65*h if h < 7 else 0
                    ii = vector.tensor_scalar(out=remp[:, h*64:(h+1)*64],
                        in0=PVL[:, off:off+64],
                        scalar1=aR2[:, h, qs:qs+1], scalar2=None, op0=ALU.mult)
                resR['rp'] = idve(ii)
            ve(_)
            t_PVLfree = resR['rp']
            t_combB_prev[0] = t_locev
            w2_, r2_ = transpose_tiles([remp[:, i*128:(i+1)*128] for i in range(4)],
                                       rempT[:], pre_wait=resR['rp'])
            tr = {}
            def _(tensor, r2_=r2_, tr=tr):
                wait(tensor, r2_, WB_r[0], WB_r[1], t_wA)
                for nh in range(2):
                    last = None
                    for kc in range(4):
                        last = tensor.matmul(WB[nh][:], rempT[:, kc, :], rout[:, kc, nh*512:(nh+1)*512],
                                             start=(kc == 0), stop=(kc == 3))
                    tr[('ro', nh)] = ipe(last)
                    WB_w[nh] = tr[('ro', nh)]
            pe(_)
            def _(vector, qs=qs, g=g, tr=tr, t_locev=t_locev, tad=t_attn_done):
                wait(vector, tr[('ro', 0)], tr[('ro', 1)], t_locev, t_sigv[g], tad)
                for h in range(NH):
                    ii = vector.scalar_tensor_tensor(out=attn[:, qs % 2, h*64:(h+1)*64],
                        in0=WB[h // 8][:, (h % 8)*64:(h % 8)*64+64],
                        scalar=sig2[:, 16+h, qs:qs+1],
                        in1=loc[:, qs % 2, h*64:(h+1)*64], op0=ALU.mult, op1=ALU.add)
                tkk = idve(ii)
                WB_r[0] = WB_r[1] = tkk
                tr['comb'] = tkk
            ve(_)
            w3_, r3_ = transpose_tiles([attn[:, qs % 2, dc*128:(dc+1)*128] for dc in range(8)],
                                       attnT[:, :, (qs % 2)*128:(qs % 2 + 1)*128],
                                       pre_wait=tr['comb'])
            t_attn_done = w3_
            resW = {}
            def _(tensor, qs=qs, r3_=r3_, resW=resW):
                wait(tensor, r3_, WB_r[0], WB_r[1], t_wA)
                for nh in range(2):
                    last = None
                    for kc in range(8):
                        last = tensor.matmul(WB[nh][:], attnT[:, kc, (qs % 2)*128:(qs % 2 + 1)*128],
                                             wout[:, kc, nh*512:(nh+1)*512], start=(kc == 0), stop=(kc == 7))
                    resW[nh] = ipe(last)
                    WB_w[nh] = resW[nh]
            pe(_)
            def _(vector, qs=qs, gq=gq, resW=resW):
                wait(vector, resW[0], resW[1], t_x2st[gq-2] if gq >= 2 else None)
                vector.tensor_tensor(out=x2buf[:, gq % 2, 0:512], in0=WB[0][:], in1=xbuf[:, gq % 4, 0:512], op=ALU.add)
                tkk = idve(vector.tensor_tensor(out=x2buf[:, gq % 2, 512:1024], in0=WB[1][:], in1=xbuf[:, gq % 4, 512:1024], op=ALU.add))
                WB_r[0] = WB_r[1] = tkk
                resW['ev'] = tkk
                t_xfree[gq] = tkk
            ve(_)
            def _(sync, gq=gq, resW=resW):
                wait(sync, resW['ev'])
                t_x2st[gq] = dinc(s_x2w[gq % 2])(sync.dma_start(out=x2_d[gq*128:(gq+1)*128, :], in_=x2buf[:, gq % 2, :]))
                nq2 = gq + 4
                if nq2 < NQ:
                    wait(sync, t_xfree[nq2 - 4])
                    t_x[nq2] = dinc(s_xld[nq2 % 4])(sync.dma_start(out=xbuf[:, nq2 % 4, :], in_=x_d[nq2*128:(nq2+1)*128, :]))
            sy(_)

    t_peA = Tok(s_pe, cnt.get("s_pe", 0))
    t_dveA = Tok(s_dve, cnt.get("s_dve", 0))
    t_actA = Tok(s_act, cnt.get("s_act", 0))
    t_poolA = Tok(s_pool, cnt.get("s_pool", 0))
    t_stA = Tok(s_gv, cnt.get("s_gv", 0))
    ctxA.close()

    B_w = [None]*6; B_r = [None]*6
    ctxB = ExitStack()
    wit = sb(ctxB,"wit_s",[128,8,F]); wot = sb(ctxB,"wot_s",[128,32,D])
    glA = sb(ctxB,"glA_s",[128,8,40]); lorB = sb(ctxB,"lorB_s",[32,F])
    wib = sb(ctxB,"wib_s",[128,32],F32); bfull = sb(ctxB,"bfull_s",[128,D],F32)
    glbrow = sb(ctxB,"glbrow_s",[1,40])
    interT = sb(ctxB,"interT",[128,4,256])
    ln2 = sb(ctxB,"ln2",[128,4,D]); ln2T = sb(ctxB,"ln2T",[128,24,256])
    mtb = sb(ctxB,"mtb",[128,4,32]); mtT = sb(ctxB,"mtT",[32,512])
    glt = stage[:, 1024:1184].rearrange("p (a b) -> p a b", a=4)
    ohb = stage[:, 1184:1216].rearrange("p (a b) -> p a b", a=4)
    mxb = stage[:, 1216:1220]
    x2ld = sb(ctxB,"x2ld",[128,6,D],F32)

    proB = {}
    def _(sync):
        wait(sync, t_peA, t_dveA, t_actA, t_poolA, t_stA, t_x2st[NQ-1], t_x2st[NQ-2])
        proB['wib'] = dinc(s_wib)(sync.dma_start(out=wib[:], in_=din["wib"]))
        proB['b'] = dinc(s_wib)(sync.dma_start(out=bfull[:], in_=din["wob"].broadcast_to([128, D])))

    sy(_)
    def _(vector):
        wait(vector, t_peA, t_actA, t_poolA, t_stA)
        vector.engine_nop()
    ve(_)
    t_wBs = cast_weights(wplan([("wit",wit,None),("wot",wot,None),("glA",glA,None),("lorB",lorB,None),("glb",glbrow,None)]), stage)

    NGB = S // 256
    t_outst = [None]*NQ
    t_ln2w = [None]*(NGB+4)
    t_lnT2ev = [None]*(NGB+4)
    t_mtev = [None]*(NGB+4)
    t_gate = [None]*(NGB+4)
    t_bias = [None]*(NGB+4)
    resG_mt = [None]*(NGB+4)
    t_mtw = [None]*(NGB+4)

    t_lnap = [[None, None] for _ in range(NGB+4)]

    def pre_ln_qs(g, qs):
        gq = 2*g + qs
        par = g % 2
        tkx = {}
        def _(sync, gq=gq, tkx=tkx):
            wait(sync, t_x2st[gq], t_outst[gq-6] if gq >= 6 else None)
            tkx['t'] = dinc(s_x2r[gq % 2])(sync.dma_start(out=x2ld[:, gq % 6, :], in_=x2_d[gq*128:(gq+1)*128, :]))
        sy(_)
        ln_toks = layer_norm(1, [x2ld[:, gq % 6, :]], [tkx['t']],
                             [ln2[:, par*2+qs, :]], stage[:, 0:1024],
                             pre=(t_ln2w[g-2] if g >= 2 else None, t_wBs))
        t_lnap[g][qs] = ln_toks[0]
        def _(gpsimd, gq=gq, g=g, qs=qs, lt=ln_toks[0]):
            wait(gpsimd, lt, proB['b'])
            tk = ipool(gpsimd.tensor_tensor(out=x2ld[:, gq % 6, :], in0=x2ld[:, gq % 6, :], in1=bfull[:], op=ALU.add))
            if qs == 1: t_bias[g] = tk
        po(_)

    def pre_pe(g):
        par = g % 2
        r3 = g % 3
        for qs in range(2):
            w_, r_ = transpose_tiles([ln2[:, par*2+qs, dc*128:(dc+1)*128] for dc in range(8)],
                                     ln2T[:, r3*8:r3*8+8, qs*128:(qs+1)*128],
                                     pre_wait=t_lnap[g][qs])
            if qs == 1:
                t_ln2w[g] = w_
                t_lnT2ev[g] = r_
        resG = {}
        def _(tensor, g=g, r3=r3, resG=resG):
            wait(tensor, t_lnT2ev[g], t_gate[g-1] if g >= 1 else None, t_wBs, t_const)
            last = None
            for qs in range(2):
                for kc in range(8):
                    tensor.matmul(PVL[:, 1024+qs*128:1024+qs*128+40], ln2T[:, r3*8+kc, qs*128:(qs+1)*128],
                                  glA[:, kc, :], start=(kc == 0), stop=False)
                last = tensor.matmul(PVL[:, 1024+qs*128:1024+qs*128+40], onesb[:], glbrow[:], start=False, stop=True)
            resG['w'] = ipe(last)
        pe(_)
        def _(vector, g=g, par=par, resG=resG):
            wait(vector, resG['w'])
            for qs in range(2):
                ii = vector.tensor_copy(out=glt[:, par*2+qs, :], in_=PVL[:, 1024+qs*128:1024+qs*128+40])
            t_gate[g] = idve(ii)
            resG['ev'] = t_gate[g]
        ve(_)
        def _(vector, g=g, par=par, resG=resG):
            wait(vector, resG['ev'], t_mtw[g-2] if g >= 2 else None)
            for qs in range(2):
                ii = vector.reduce_max(out=mxb[:, par*2+qs:par*2+qs+1], in_=glt[:, par*2+qs, 0:8], axis=AX.X)
            t = idve(ii); vector.wait_ge(t.sem, t.val)
            for qs in range(2):
                ii = vector.tensor_scalar(out=ohb[:, par*2+qs, :], in0=glt[:, par*2+qs, 0:8], scalar1=mxb[:, par*2+qs:par*2+qs+1], scalar2=None, op0=ALU.is_equal)
            t = idve(ii); vector.wait_ge(t.sem, t.val)
            for qs in range(2):
                ii = vector.tensor_tensor(out=mtb[:, par*2+qs, :].rearrange("p (a b) -> p a b", a=8),
                                     in0=glt[:, par*2+qs, 8:40].rearrange("p (a b) -> p a b", a=8),
                                     in1=ohb[:, par*2+qs, :].unsqueeze(2).broadcast_to([128, 8, 4]), op=ALU.mult)
            resG['mt'] = idve(ii)
        ve(_)
        resG_mt[g] = resG['mt']

    def mtb_t(g):
        par = g % 2
        w_, ev_mt = transpose_tiles([mtb[:, par*2, :], mtb[:, par*2+1, :]], mtT[:, par*256:par*256+256], pre_wait=resG_mt[g], rows=32)
        t_mtev[g] = ev_mt
        t_mtw[g] = w_

    def body(g):
        gq0 = 2*g
        par = g % 2
        relu_toks = [None]*32
        wo_toks = [None]*32
        def wo_out(k):
            return WB[k][:, 0:512] if k < 2 else PVL[:, (k-2)*512:(k-1)*512]
        resW = {}
        def emit_wo(mtn):
            def _(tensor, mtn=mtn):
                need = [relu_toks[mtn], t_wBs]
                if mtn == 0:
                    need += [B_r[2], B_r[3], B_r[4], B_r[5]]
                wait(tensor, *need)
                tkk = None
                for qs in range(2):
                    for nh in range(2):
                        tkk = tensor.matmul(wo_out(qs*2 + nh), interT[:, mtn % 4, qs*128:(qs+1)*128],
                                            wot[:, mtn, nh*512:(nh+1)*512], start=(mtn == 0), stop=(mtn == 31))
                wo_toks[mtn] = ipe(tkk)
                if mtn == 31:
                    resW['wo'] = wo_toks[31]
                    for i in (2,3,4,5): B_w[i] = wo_toks[31]
            pe(_)
        for mtn in range(32):
            bb = mtn % 2
            resB = {}
            def _(tensor, mtn=mtn, bb=bb, resB=resB, g=g, par=par):
                wait(tensor, B_r[bb], t_lnT2ev[g], t_mtev[g], t_wBs)
                r3 = g % 3
                last = None
                for kc in range(8):
                    last = tensor.matmul(SC[bb][:, 0:256], wit[:, kc, mtn*128:(mtn+1)*128], ln2T[:, r3*8+kc, :],
                                         start=(kc == 0), stop=False)
                last = tensor.matmul(SC[bb][:, 0:256], lorB[:, mtn*128:(mtn+1)*128], mtT[:, par*256:par*256+256], start=False, stop=True)
                resB['w'] = ipe(last); B_w[bb] = resB['w']
            pe(_)
            def _(vector, mtn=mtn, bb=bb, resB=resB, relu_toks=relu_toks, wo_toks=wo_toks):
                wait(vector, resB['w'], proB['wib'], wo_toks[mtn-4] if mtn >= 4 else None)
                B_r[bb] = idve(vector.tensor_scalar(out=interT[:, mtn % 4, :], in0=SC[bb][:, 0:256],
                                                    scalar1=wib[:, mtn:mtn+1], scalar2=0.0,
                                                    op0=ALU.add, op1=ALU.max))
                relu_toks[mtn] = B_r[bb]
            ve(_)
            if mtn >= 2:
                emit_wo(mtn - 2)
        emit_wo(30)
        emit_wo(31)
        def _(vector, resW=resW, g=g, gq0=gq0):
            wait(vector, resW['wo'], t_bias[g])
            for qs in range(2):
                for nh in range(2):
                    ii = vector.tensor_tensor(out=x2ld[:, (gq0+qs) % 6, nh*512:(nh+1)*512], in0=wo_out(qs*2 + nh),
                                         in1=x2ld[:, (gq0+qs) % 6, nh*512:(nh+1)*512], op=ALU.add)
            tkk = idve(ii)
            for i in (2,3,4,5): B_r[i] = tkk
            resW['ev'] = tkk
        ve(_)
        def _(sync, gq0=gq0, resW=resW):
            wait(sync, resW['ev'])
            for qs in range(2):
                gq = gq0 + qs
                t_outst[gq] = dinc(s_outw[gq % 2])(sync.dma_start(out=out_d[gq*128:(gq+1)*128, :], in_=x2ld[:, gq % 6, :]))
        sy(_)

    for g in (0, 1):
        pre_ln_qs(g, 0)
        pre_ln_qs(g, 1)
        pre_pe(g)
    for g in range(NGB):
        mtb_t(g)
        if g + 2 < NGB:
            pre_ln_qs(g + 2, 0)
            pre_ln_qs(g + 2, 1)
        body(g)
        if g + 2 < NGB:
            pre_pe(g + 2)

    def _(gpsimd):
        wait(gpsimd, *[t_outst[gq] for gq in range(NQ)])
    po(_)
    ctxB.close()
    ctx.close()
    _group_blocks_by_engine(nc)
    return nc

def _group_blocks_by_engine(nc):
    pats = ("_SP_", "_Activation_", "_PE_", "_DVE_", "_Pool_")
    f = list(nc.m.functions)[0]
    blocks = list(f.blocks)
    if len(blocks) < 3:
        return
    mid = blocks[1:-1]
    def key(b):
        for j, p in enumerate(pats):
            if p in b.name:
                return j
        return None
    keys = [key(b) for b in mid]
    if any(k is None for k in keys):
        return
    f.blocks = [blocks[0]] + [b for _, b in sorted(zip(keys, mid), key=lambda t: t[0])] + [blocks[-1]]

def _prep(inputs):
    f32 = np.float32
    g = lambda k: np.ascontiguousarray(np.asarray(inputs[k], f32))
    w = {}
    w["wq"] = g("L0_attn_wq"); w["wk"] = g("L0_attn_wk"); w["wv"] = g("L0_attn_wv")
    w["wgq"] = g("L0_attn_wgq"); w["wgk"] = g("L0_attn_wgk"); w["wgv"] = g("L0_attn_wgv")
    w["rout"] = g("L0_attn_r_out_w"); w["wout"] = g("L0_attn_w_out")
    glw = g("L0_attn_gate_local_w"); grw = g("L0_attn_gate_remote_w")
    gb = np.zeros((D, 32), f32)
    for h in range(NH):
        gb[h*64:(h+1)*64, h] = glw[h]
        gb[h*64:(h+1)*64, 16+h] = grw[h]
    w["gblk"] = gb
    w["wit"] = np.ascontiguousarray(g("L0_ffn_wi_W").T)
    w["wot"] = np.ascontiguousarray(g("L0_ffn_wo_W").T)
    gA = np.concatenate([g("L0_ffn_gate_W").T, g("L0_ffn_lora_A").reshape(32, D).T], axis=1)
    w["glA"] = np.ascontiguousarray(gA)
    glb = np.zeros((1, 40), f32); glb[0, 0:8] = g("L0_ffn_gate_b")
    w["glb"] = glb
    w["lorB"] = np.ascontiguousarray(g("L0_ffn_lora_B").transpose(0, 2, 1).reshape(32, F))
    w["wib"] = np.ascontiguousarray(g("L0_ffn_wi_b").reshape(32, 128).T)
    w["wob"] = g("L0_ffn_wo_b").reshape(1, D)
    bf = ml_dtypes.bfloat16
    w["identc"] = np.eye(128, dtype=bf)
    kk = np.arange(128)[:, None]; qq = np.arange(128)[None, :]
    M0 = np.where(kk > qq, 1.0, 0.0).astype(np.float16)
    M2 = np.where(kk <= qq, 1.0, 0.0).astype(np.float16)
    w["maskc"] = np.ascontiguousarray(np.concatenate([M0, np.ones((128, 128), np.float16), M2], axis=1))
    w["onesb"] = np.ones((1, 128), bf)
    return w

def kernel(**inputs):
    x = np.ascontiguousarray(np.asarray(inputs["L0_attn_x"], np.float32))
    Bn, S, _ = x.shape
    w = _prep(inputs)
    nc = build_program(S=S)
    in_maps = []
    for b in range(Bn):
        m = dict(w); m["x"] = x[b]
        in_maps.append(m)
    res = run_bass_kernel_spmd(nc, in_maps, core_ids=list(range(Bn)))
    return np.stack([np.asarray(r["out"], np.float32) for r in res.results], axis=0)


# version 19
# speedup vs baseline: 1.2006x; 1.2006x over previous
import numpy as np
import ml_dtypes
from contextlib import ExitStack

import concourse.bass as bass
import concourse.mybir as mybir
from concourse.bass_utils import run_bass_kernel_spmd

F32 = mybir.dt.float32
BF16 = mybir.dt.bfloat16
F16 = mybir.dt.float16
AF = mybir.ActivationFunctionType
ALU = mybir.AluOpType
AX = mybir.AxisListType

D, NH, HD, LH, LD, F = 1024, 16, 64, 8, 512, 4096
NEG = -65536.0
EPS = 1e-5

def _pvoff(h):
    return 512 * (h // 7) + 65 * (h % 7)

class Tok:
    __slots__ = ("sem", "val")
    def __init__(self, sem, val): self.sem, self.val = sem, val

def build_program(S=8192):
    NG = S // 512
    NQ = S // 128
    nc = bass.Bass("TRN2", target_bir_lowering=False, debug=False)
    cnt = {}
    def mk_inc(sem):
        def inc(instr, n=1):
            cnt[sem.name] = cnt.get(sem.name, 0) + n
            instr.then_inc(sem, n)
            return Tok(sem, cnt[sem.name])
        return inc
    def wait(eng, *toks):
        best = {}
        for t in toks:
            if t is None: continue
            k = t.sem.name
            if k not in best or best[k].val < t.val: best[k] = t
        for t in best.values(): eng.wait_ge(t.sem, t.val)

    x_d = nc.dram_tensor("x", [S, D], F32, kind="ExternalInput").ap()
    out_d = nc.dram_tensor("out", [S, D], F32, kind="ExternalOutput").ap()
    x2_d = nc.dram_tensor("x2s", [S, D], F32, kind="Internal").ap()
    din = {}
    for nm, shp in [("wq",[D,D]),("wk",[D,D]),("wv",[D,D]),("wgq",[D,LD]),("wgk",[D,LD]),
                    ("wgv",[D,LD]),("rout",[LD,D]),("wout",[D,D]),("gblk",[D,32]),
                    ("wit",[D,F]),("wot",[F,D]),("glA",[D,40]),("glb",[1,40]),
                    ("lorB",[32,F]),("wib",[128,32]),("wob",[1,D])]:
        din[nm] = nc.dram_tensor(nm, shp, F32, kind="ExternalInput").ap()
    witb_d = nc.dram_tensor("witb", [128, 8*F], BF16, kind="Internal").ap()
    wotb_d = nc.dram_tensor("wotb", [128, 32*D], BF16, kind="Internal").ap()
    ident_d = nc.dram_tensor("identc", [128,128], BF16, kind="ExternalInput").ap()
    mask_d = nc.dram_tensor("maskc", [128,384], F16, kind="ExternalInput").ap()
    onesb_d = nc.dram_tensor("onesb", [1,128], BF16, kind="ExternalInput").ap()

    ctx = ExitStack()
    ctxA = ExitStack()
    def sb(stk, name, shape, dtype=BF16): return stk.enter_context(nc.sbuf_tensor(name, shape, dtype))
    def ps(name, shape, dtype=F32): return ctx.enter_context(nc.psum_tensor(name, shape, dtype))

    ident = sb(ctx,"ident_s",[128,128]); maskc = sb(ctx,"maskc_s",[128,384],F16)
    onesb = sb(ctx,"onesb_s",[1,128])
    stats = sb(ctx,"stats_s",[128,32],F32)
    stage = sb(ctx,"stage",[128,2048],F32)

    xbuf = sb(ctxA,"xbuf",[128,4,D],F32)
    x2buf = sb(ctxA,"x2buf",[128,2,D],F32)
    wq = sb(ctxA,"wq_s",[128,8,D]); wk = sb(ctxA,"wk_s",[128,8,D]); wv = sb(ctxA,"wv_s",[128,8,D])
    wout = sb(ctxA,"wout_s",[128,8,D]); wgq = sb(ctxA,"wgq_s",[128,8,LD])
    wgk = sb(ctxA,"wgk_s",[128,8,LD]); wgv = sb(ctxA,"wgv_s",[128,8,LD])
    rout = sb(ctxA,"rout_s",[128,4,D]); gblk = sb(ctxA,"gblk_s",[128,8,32])
    ln1 = sb(ctxA,"ln1",[128,4,D])
    ln1T = sb(ctxA,"ln1T",[128,8,512])
    QT = sb(ctxA,"QT",[128,8,512])
    KT = sb(ctxA,"KT",[128,8,768])
    VV = sb(ctxA,"VV",[128,6,16*65],F16)
    VVv = VV[:].rearrange("p s (h c) -> p s h c", h=16)
    GQT = sb(ctxA,"GQT",[128,4,512])
    PL = sb(ctxA,"PL",[128,7,384],F16)
    PRL = sb(ctxA,"PRL",[64,8,128],F16)
    loc = sb(ctxA,"loc",[128,2,D])
    attn = ln1
    attnT = sb(ctxA,"attnT",[128,8,256])
    remp = sb(ctxA,"remp",[128,LD]); rempT = sb(ctxA,"rempT",[128,4,128])
    pooledT = sb(ctxA,"pooledT",[128,8,64])
    GKT = sb(ctxA,"GKT",[128,4,64]); GVA = sb(ctxA,"GVA",[64,8,65],F16); GVs = sb(ctxA,"GVs",[4,LD])
    gq_g = sb(ctxA,"gq_g",[128,4,32],F32)
    sig2 = sb(ctxA,"sig2",[128,32,4],F32)
    aL2 = sb(ctxA,"aL2",[128,16,4],F32)
    aR2 = sb(ctxA,"aR2",[128,8,4],F32)

    T = ps("T",[128,1024],BF16)
    SC = [ps(f"SC{i}",[128,512],F32) for i in range(2)]
    PVL = ps("PVL",[128,1536],F32)
    _pvv = PVL[:].rearrange("p (b c) -> p b c", b=3)
    pv_den = [_pvv[:, 0, 0:455].rearrange("p (h c) -> p h c", h=7)[:, :, 64:65],
              _pvv[:, 1, 0:455].rearrange("p (h c) -> p h c", h=7)[:, :, 64:65],
              _pvv[:, 2, 0:130].rearrange("p (h c) -> p h c", h=2)[:, :, 64:65]]
    WB = [ps(f"WB{i}",[128,512],F32) for i in range(2)]

    s_pe = ctx.enter_context(nc.semaphore("s_pe"))
    s_dve = ctx.enter_context(nc.semaphore("s_dve"))
    s_act = ctx.enter_context(nc.semaphore("s_act"))
    s_pool = ctx.enter_context(nc.semaphore("s_pool"))
    s_const = ctx.enter_context(nc.semaphore("s_const"))
    s_xld = [ctx.enter_context(nc.semaphore(f"s_xld{i}")) for i in range(4)]
    s_x2w = [ctx.enter_context(nc.semaphore(f"s_x2w{i}")) for i in range(2)]
    s_wc = [ctx.enter_context(nc.semaphore(f"s_wc{i}")) for i in range(4)]
    s_wib = ctx.enter_context(nc.semaphore("s_wib"))
    s_x2r = [ctx.enter_context(nc.semaphore(f"s_x2r{i}")) for i in range(2)]
    s_outw = [ctx.enter_context(nc.semaphore(f"s_outw{i}")) for i in range(2)]
    s_gv = ctx.enter_context(nc.semaphore("s_gv"))
    s_pci = ctx.enter_context(nc.semaphore("s_pci"))
    s_pco = ctx.enter_context(nc.semaphore("s_pco"))
    ipe, idve, iact, ipool = (mk_inc(s) for s in (s_pe,s_dve,s_act,s_pool))
    _dma_inc = {}
    def dinc(sem):
        if sem.name not in _dma_inc: _dma_inc[sem.name] = mk_inc(sem)
        return lambda instr: _dma_inc[sem.name](instr, 16)

    blk = ctx.enter_context(nc.Block())
    def pe(f): blk.tensor(f)
    def ve(f): blk.vector(f)
    def ac(f): blk.scalar(f)
    def po(f): blk.gpsimd(f)
    def sy(f): blk.sync(f)

    t_x = [None]*(NQ+8)
    t_xfree = [None]*(NQ+8)
    t_x2st = [None]*NQ
    TB_w = [None]; TB_r = [None]
    WB_w = [None, None]; WB_r = [None, None]

    pro = {}
    def _(sync):
        dinc(s_const)(sync.dma_start(out=ident[:], in_=ident_d))
        dinc(s_const)(sync.dma_start(out=maskc[:], in_=mask_d))
        pro['c2'] = dinc(s_const)(sync.dma_start(out=onesb[:], in_=onesb_d))
        for gq in range(min(4, NQ)):
            t_x[gq] = dinc(s_xld[gq % 4])(sync.dma_start(out=xbuf[:, gq % 4, :], in_=x_d[gq*128:(gq+1)*128, :]))
    sy(_)
    vones = {}
    def _(vector):
        vector.memset(VVv[:, :, :, 64:65], 1.0)
        vones['v'] = idve(vector.memset(GVA[:, :, 64:65], 1.0))
    ve(_)
    t_vones = vones['v']
    def cast_weights(plan, stg):
        nslot = stg.shape[1] // 1024
        slots = [None]*nslot
        last = None
        for i, (src, dstv, sc, P) in enumerate(plan):
            h = i % nslot
            prev = slots[h]
            tk = {}
            n = src.shape[-1]
            def _(eng, src=src, h=h, prev=prev, n=n, P=P, tk=tk):
                if prev is not None: wait(eng, prev)
                tk['ld'] = dinc(s_wc[h])(eng.dma_start(out=stg[0:P, h*1024:h*1024+n], in_=src))
            (sy if (nslot == 2 or i % 2 == 0) else po)(_)
            def _(vector, tk=tk, dstv=dstv, sc=sc, n=n, h=h, P=P):
                wait(vector, tk['ld'])
                srcv = stg[0:P, h*1024:h*1024+n]
                if sc is None:
                    tk['c'] = idve(vector.tensor_copy(out=dstv, in_=srcv))
                else:
                    tk['c'] = idve(vector.tensor_scalar(out=dstv, in0=srcv, scalar1=float(sc), scalar2=None, op0=ALU.mult))
            ve(_)
            slots[h] = tk['c']
            last = tk['c']
        return last
    def wplan(names_dsts):
        plan = []
        for nm, dst, sc in names_dsts:
            d = din[nm]; rows, cols = d.shape
            dstv = dst[:].rearrange("p a b -> p (a b)") if len(dst.shape) == 3 else dst[:]
            if rows >= 128:
                view = d.rearrange("(c p) n -> p c n", p=128)
                nch = rows // 128
                for c in range(nch):
                    for off in range(0, cols, 1024):
                        n = min(1024, cols - off)
                        plan.append((view[:, c, off:off+n], dstv[:, c*cols+off:c*cols+off+n], sc, 128))
            else:
                for off in range(0, cols, 1024):
                    n = min(1024, cols - off)
                    plan.append((d[0:rows, off:off+n], dstv[0:rows, off:off+n], sc, rows))
        return plan
    t_wA = cast_weights(wplan([("wq",wq,None),("wk",wk,0.125),("wv",wv,None),("wout",wout,None),
                               ("wgq",wgq,None),("wgk",wgk,1.0/1024),("wgv",wgv,1.0/128),
                               ("rout",rout,None),("gblk",gblk,None)]), stage)
    t_const = pro['c2']

    def transpose_tiles(srcs, dst_evac, pre_wait=None, rows=128):
        res = {}
        def _(tensor):
            wait(tensor, TB_r[0], pre_wait, t_const)
            ncols = 0; last = None
            for s in srcs:
                w0 = s.shape[1]
                last = tensor.transpose(T[0:w0, ncols:ncols+128], s, ident[:])
                ncols += 128
            res['w'] = ipe(last); TB_w[0] = res['w']
        pe(_)
        def _(scalar):
            wait(scalar, res['w'])
            ncols = 128*len(srcs)
            srcap = T[0:rows, 0:ncols]
            if len(dst_evac.shape) == 3:
                srcap = srcap.rearrange("p (a b) -> p a b", a=dst_evac.shape[1])
            TB_r[0] = iact(scalar.copy(out=dst_evac, in_=srcap))
            res['r'] = TB_r[0]
        ac(_)
        return res['w'], res['r']

    def layer_norm(nqs, srcs, src_toks, dsts, scratch, pre=()):
        toks = {}
        def _(vector):
            wait(vector, *pre, *src_toks)
            for qs in range(nqs):
                ii = vector.reduce_sum(out=stats[:, qs:qs+1], in_=srcs[qs], axis=AX.X)
            toks['s1'] = idve(ii)
        ve(_)
        def _(scalar):
            wait(scalar, *pre)
            for qs in range(nqs):
                wait(scalar, src_toks[qs])
                scalar.activation(out=scratch, in_=srcs[qs], func=AF.Square, accum_out=stats[:, 4+qs:5+qs])
            wait(scalar, toks['s1'])
            scalar.activation(out=stats[:, 8:8+nqs], in_=stats[:, 0:nqs], func=AF.Copy, scale=-1.0/D)
            toks['nm'] = iact(scalar.activation(out=stats[:, 12:12+nqs], in_=stats[:, 0:nqs], func=AF.Square, scale=1.0/D))
        ac(_)
        def _(vector):
            wait(vector, toks['nm'])
            t = idve(vector.tensor_scalar(out=stats[:, 16:16+nqs], in0=stats[:, 4:4+nqs], scalar1=1.0/D, scalar2=EPS, op0=ALU.mult, op1=ALU.add))
            vector.wait_ge(t.sem, t.val)
            toks['var'] = idve(vector.tensor_tensor(out=stats[:, 16:16+nqs], in0=stats[:, 16:16+nqs], in1=stats[:, 12:12+nqs], op=ALU.subtract))
        ve(_)
        def _(scalar):
            wait(scalar, toks['var'])
            t = iact(scalar.activation(out=stats[:, 20:20+nqs], in_=stats[:, 16:16+nqs], func=AF.Ln))
            scalar.wait_ge(t.sem, t.val)
            toks['rstd'] = iact(scalar.activation(out=stats[:, 24:24+nqs], in_=stats[:, 20:20+nqs], func=AF.Exp, scale=-0.5))
        ac(_)
        out = []
        def _(vector):
            wait(vector, toks['rstd'])
            for qs in range(nqs):
                out.append(idve(vector.tensor_scalar(out=dsts[qs], in0=srcs[qs],
                    scalar1=stats[:, 8+qs:9+qs], scalar2=stats[:, 24+qs:25+qs],
                    op0=ALU.add, op1=ALU.mult)))
        ve(_)
        return out

    pc_pieces = []
    _vw = din["wit"].rearrange("(c p) n -> p c n", p=128)
    for c in range(8):
        for off in range(0, F, 1024):
            pc_pieces.append((_vw[:, c, off:off+1024], witb_d[:, c*F+off:c*F+off+1024]))
    _vo = din["wot"].rearrange("(c p) n -> p c n", p=128)
    for c in range(32):
        pc_pieces.append((_vo[:, c, :], wotb_d[:, c*D:(c+1)*D]))
    pc_state = {'in': None, 'cast': None, 'out': None}

    t_prev_ln_done = None
    t_attn_done = [None]
    t_pool = [None]*NG
    t_GKGV = [None]*NG
    t_GKT_ev = [None]*NG
    t_QTev = [None]*NG
    t_KTev = [None]*NG
    t_Vev = [None]*NQ
    t_GQTev = [None]*NG
    t_scdone = [None]*(NQ+8)
    t_rscdone = [None]*(NQ+8)
    t_pvdone = [None]*(NQ+8)
    t_sigv = [None]*NG
    sc_hist = [None, None]
    pl_hist = [None]*7
    scnt = [0]
    prl_hist = [None]*8
    t_PVLfree = [None]
    t_combB_prev = [None]

    for g in range(NG):
        gq0 = 4*g
        srcs = [xbuf[:, (gq0+qs) % 4, :] for qs in range(4)]
        stoks = [t_x[gq0+qs] for qs in range(4)]
        ln_toks = layer_norm(4, srcs, stoks, [ln1[:, qs, :] for qs in range(4)], stage[:, 0:1024],
                             pre=(t_prev_ln_done, t_attn_done[0], t_wA))
        lnT_evs = []
        for qs in range(4):
            w_, r_ = transpose_tiles([ln1[:, qs, dc*128:(dc+1)*128] for dc in range(8)],
                                     ln1T[:, :, qs*128:(qs+1)*128],
                                     pre_wait=ln_toks[qs])
            lnT_evs.append(r_)
            if qs == 3: t_prev_ln_done = w_
        t_lnT = lnT_evs[-1]
        def _(vector, g=g, evs=tuple(lnT_evs)):
            wait(vector, *evs)
            with nc.allow_low_precision(reason="pooled sums feed bf16 matmuls"):
                t_pool[g] = idve(vector.tensor_reduce(
                    out=pooledT[:, :, 4*g:4*g+4],
                    in_=ln1T[:].rearrange("p a (c t) -> p a c t", c=4), axis=AX.X, op=ALU.add))
        ve(_)
        t_qfree = t_scdone[gq0-1] if g > 0 else None
        t_gqfree = t_rscdone[gq0-1] if g > 0 else None
        qk_ev = {}
        for proj, wsb in (("q", wq), ("k", wk)):
            for mt in range(8):
                bidx = mt % 2
                res = {}
                def _(tensor, wsb=wsb, mt=mt, bidx=bidx, res=res):
                    wait(tensor, WB_r[bidx], t_lnT, t_wA)
                    last = None
                    for kc in range(8):
                        last = tensor.matmul(WB[bidx][:], wsb[:, kc, mt*128:(mt+1)*128],
                                             ln1T[:, kc, :], start=(kc == 0), stop=(kc == 7))
                    res['w'] = ipe(last); WB_w[bidx] = res['w']
                pe(_)
                if proj == "q":
                    def _(vector, mt=mt, bidx=bidx, res=res, t_qfree=t_qfree):
                        wait(vector, res['w'], t_qfree)
                        WB_r[bidx] = idve(vector.tensor_copy(out=QT[:, mt, :], in_=WB[bidx][:]))
                        qk_ev[('q', mt)] = WB_r[bidx]
                    ve(_)
                else:
                    def _(scalar, mt=mt, bidx=bidx, res=res, t_qfree=t_qfree, gq0=gq0):
                        wait(scalar, res['w'], t_qfree)
                        s0 = gq0 % 6
                        if s0 <= 2:
                            ii = scalar.copy(out=KT[:, mt, s0*128:(s0+4)*128], in_=WB[bidx][:])
                        else:
                            w1 = (6 - s0)*128
                            scalar.copy(out=KT[:, mt, s0*128:768], in_=WB[bidx][:, 0:w1])
                            ii = scalar.copy(out=KT[:, mt, 0:512-w1], in_=WB[bidx][:, w1:512])
                        WB_r[bidx] = iact(ii)
                        qk_ev[('k', mt)] = WB_r[bidx]
                    ac(_)
        t_QTev[g] = qk_ev[('q', 7)]
        t_KTev[g] = qk_ev[('k', 7)]
        for qs in range(4):
            gq = gq0 + qs
            for nh in range(2):
                bidx = nh
                res = {}
                def _(tensor, qs=qs, nh=nh, bidx=bidx, res=res):
                    wait(tensor, WB_r[bidx], t_lnT, t_wA)
                    last = None
                    for kc in range(8):
                        last = tensor.matmul(WB[bidx][:], ln1T[:, kc, qs*128:(qs+1)*128],
                                             wv[:, kc, nh*512:(nh+1)*512], start=(kc == 0), stop=(kc == 7))
                    res['w'] = ipe(last); WB_w[bidx] = res['w']
                pe(_)
                def _(vector, gq=gq, nh=nh, bidx=bidx, res=res):
                    wait(vector, res['w'], t_pvdone[gq-4] if gq >= 4 else None)
                    WB_r[bidx] = idve(vector.tensor_copy(
                        out=VVv[:, gq % 6, 8*nh:8*nh+8, 0:64],
                        in_=WB[bidx][:].rearrange("p (h d) -> p h d", h=8)))
                    res['ev'] = WB_r[bidx]
                ve(_)
                if nh == 1: t_Vev[gq] = res['ev']
        for mt in range(4):
            bidx = mt % 2
            res = {}
            def _(tensor, mt=mt, bidx=bidx, res=res):
                wait(tensor, WB_r[bidx], t_lnT, t_wA)
                last = None
                for kc in range(8):
                    last = tensor.matmul(WB[bidx][:], wgq[:, kc, mt*128:(mt+1)*128],
                                         ln1T[:, kc, :], start=(kc == 0), stop=(kc == 7))
                res['w'] = ipe(last); WB_w[bidx] = res['w']
            pe(_)
            def _(scalar, mt=mt, bidx=bidx, res=res, t_gqfree=t_gqfree):
                wait(scalar, res['w'], t_gqfree)
                WB_r[bidx] = iact(scalar.copy(out=GQT[:, mt, :], in_=WB[bidx][:]))
                res['ev'] = WB_r[bidx]
            ac(_)
            if mt == 3: t_GQTev[g] = res['ev']
        resG = {}
        def _(tensor, resG=resG, g=g):
            wait(tensor, WB_r[0], t_QTev[g], t_wA)
            last = None
            for qs in range(4):
                for kc in range(8):
                    last = tensor.matmul(WB[0][:, qs*128:qs*128+32], QT[:, kc, qs*128:(qs+1)*128],
                                         gblk[:, kc, :], start=(kc == 0), stop=(kc == 7))
            resG['w'] = ipe(last); WB_w[0] = resG['w']
        pe(_)
        def _(vector, resG=resG):
            wait(vector, resG['w'])
            WB_r[0] = idve(vector.tensor_copy(
                out=gq_g[:],
                in_=WB[0][:].rearrange("p (a b) -> p a b", a=4)[:, :, 0:32]))
            resG['ev'] = WB_r[0]
        ve(_)
        def _(scalar, resG=resG):
            wait(scalar, resG['ev'])
            resG['e'] = iact(scalar.activation(out=sig2[:].transpose([0, 2, 1]),
                                               in_=gq_g[:], func=AF.Exp, scale=-1.0))
        ac(_)
        def _(vector, resG=resG, g=g):
            wait(vector, resG['e'])
            t = idve(vector.tensor_scalar(out=sig2[:], in0=sig2[:],
                                 scalar1=1.0, scalar2=None, op0=ALU.add))
            vector.wait_ge(t.sem, t.val)
            t_sigv[g] = idve(vector.reciprocal(out=sig2[:], in_=sig2[:]))
        ve(_)
        resK = {}
        def _(tensor, g=g, resK=resK):
            wait(tensor, t_pool[g], resK_prev['ev'] if g > 0 else None, t_wA)
            last = None
            for mt in range(4):
                for kc in range(8):
                    last = tensor.matmul(SC[1][:, 384+mt*4:384+mt*4+4], wgk[:, kc, mt*128:(mt+1)*128],
                                         pooledT[:, kc, 4*g:4*g+4], start=(kc == 0), stop=(kc == 7))
            resK['w'] = ipe(last)
        pe(_)
        def _(vector, g=g, resK=resK):
            wait(vector, resK['w'])
            for mt in range(4):
                ii = vector.tensor_copy(out=GKT[:, mt, 4*g:4*g+4], in_=SC[1][:, 384+mt*4:384+mt*4+4])
            resK['ev'] = idve(ii)
            t_GKT_ev[g] = resK['ev']
        ve(_)
        resK_prev = resK
        resV = {}
        def _(tensor, g=g, resV=resV):
            wait(tensor, t_pool[g], WB_r[1], t_wA)
            last = None
            for kc in range(8):
                last = tensor.matmul(WB[1][0:4, :], pooledT[:, kc, 4*g:4*g+4],
                                     wgv[:, kc, :], start=(kc == 0), stop=(kc == 7))
            resV['w'] = ipe(last); WB_w[1] = resV['w']
        pe(_)
        def _(vector, g=g, resV=resV):
            wait(vector, resV['w'], t_GKGV[g-1] if g > 0 else None)
            WB_r[1] = idve(vector.tensor_copy(out=GVs[:], in_=WB[1][0:4, :]))
            resV['ev'] = WB_r[1]
        ve(_)
        def _(gpsimd, g=g, resV=resV):
            wait(gpsimd, resV['ev'])
            t_GKGV[g] = dinc(s_gv)(gpsimd.dma_start(
                out=GVA[4*g:4*g+4, :, 0:64],
                in_=GVs[:].rearrange("p (h d) -> p h d", h=8)))
        po(_)

        for pk in range(4*g, min(4*g+4, 0)):
            srcp, dstp = pc_pieces[pk]
            def _(sync, srcp=srcp):
                wait(sync, pc_state['cast'] if pc_state['cast'] is not None else t_wA)
                pc_state['in'] = dinc(s_pci)(sync.dma_start(out=stage[:, 1024:2048], in_=srcp))
            sy(_)
            def _(vector):
                wait(vector, pc_state['in'], pc_state['out'], t_wA)
                pc_state['cast'] = idve(vector.tensor_copy(out=stg16[:], in_=stage[:, 1024:2048]))
            ve(_)
            def _(sync, dstp=dstp):
                wait(sync, pc_state['cast'])
                pc_state['out'] = dinc(s_pco)(sync.dma_start(out=dstp, in_=stg16[:]))
            sy(_)

        def _plslot(h):
            return {0: 4, 14: 5, 15: 6}.get(h, h % 4)

        def qsub_stages(g, qs, gq0=gq0):
            gq = gq0 + qs
            kts = tuple(k for k in (gq-2, gq-1, gq) if k >= 0)
            nk = len(kts)
            nl = min(gq + 1, 64)
            S = {}
            sc_toks = {}; exp_toks = {}; msk_toks = {}; pv_toks = {}
            rsc_toks = {}; rexp_toks = {}; rpv_toks = {}
            def bank_of(k):
                return SC[k] if k < 2 else WB[k-2]
            def emit_score(h):
                sc = scnt[0] % 4; scnt[0] += 1
                rb = (h % 2)*64; hp = h // 2
                res = {'bk': sc}
                def _(tensor, h=h, sc=sc, rb=rb, hp=hp, res=res):
                    wait(tensor, sc_hist[sc] if sc < 2 else WB_r[sc-2], t_QTev[g], t_KTev[g],
                         t_KTev[g-1] if (g > 0 and kts[0] < 4*g) else None, t_const)
                    bk = bank_of(sc)
                    last = None
                    for ci, kq in enumerate(kts):
                        last = tensor.matmul(bk[:, ci*128:(ci+1)*128],
                            KT[rb:rb+64, hp, (kq % 6)*128:(kq % 6)*128+128],
                            QT[rb:rb+64, hp, qs*128:(qs+1)*128],
                            start=(ci == 0), stop=(ci == nk-1),
                            tile_position=(rb, 0))
                    res['w'] = ipe(last)
                pe(_)
                return res
            def emit_exp(h):
                sc = sc_toks[h]['bk']; sl = _plslot(h)
                res = {}
                def _(scalar, h=h, sc=sc, sl=sl, res=res):
                    wait(scalar, sc_toks[h]['w'], pl_hist[sl])
                    bk = bank_of(sc)
                    res['e'] = iact(scalar.activation(out=PL[:, sl, 0:nk*128], in_=bk[:, 0:nk*128], func=AF.Exp))
                    if sc < 2: sc_hist[sc] = res['e']
                    else: WB_r[sc-2] = res['e']
                ac(_)
                return res
            def emit_msk(h):
                sl = _plslot(h)
                res = {}
                def _(vector, h=h, sl=sl, res=res):
                    wait(vector, exp_toks[h]['e'], t_const)
                    res['m'] = idve(vector.tensor_tensor(out=PL[:, sl, 0:nk*128], in0=PL[:, sl, 0:nk*128],
                                                         in1=maskc[:, (3-nk)*128:384], op=ALU.mult))
                ve(_)
                return res
            def emit_pv(h):
                sl = _plslot(h)
                res = {}
                def _(tensor, h=h, sl=sl, res=res):
                    need = [msk_toks[h]['m'], t_vones]
                    for kq in kts: need.append(t_Vev[kq])
                    if h == 0 or h >= 14: need.append(t_PVLfree[0])
                    elif h == 1: need.append(t_combB_prev[0])
                    wait(tensor, *need)
                    off = _pvoff(h)
                    last = None
                    for ci, kq in enumerate(kts):
                        last = tensor.matmul(PVL[:, off:off+65],
                            PL[:, sl, ci*128:(ci+1)*128],
                            VVv[:, kq % 6, h, :],
                            start=(ci == 0), stop=(ci == nk-1))
                    res['w'] = ipe(last)
                    pl_hist[sl] = res['w']
                pe(_)
                return res

            def stA():
                for h in range(NH):
                    sc_toks[h] = emit_score(h)
                    exp_toks[h] = emit_exp(h)
                    msk_toks[h] = emit_msk(h)
                    if h >= 3 and 1 <= h-2 <= 13:
                        pv_toks[h-2] = emit_pv(h-2)
                t_scdone[gq] = sc_toks[NH-1]['w']
            def stB():
                for h in (0, 14, 15):
                    pv_toks[h] = emit_pv(h)
                t_pvdone[gq] = pv_toks[15]['w']
            def stC():
                resL = S
                def _(vector, resL=resL):
                    wait(vector, pv_toks[14]['w'], pv_toks[15]['w'], t_sigv[g])
                    t = idve(vector.reciprocal(out=aL2[:, 14:16, qs:qs+1], in_=pv_den[2]))
                    vector.wait_ge(t.sem, t.val)
                    t = idve(vector.tensor_tensor(out=aL2[:, 14:16, qs:qs+1], in0=aL2[:, 14:16, qs:qs+1],
                                                  in1=sig2[:, 14:16, qs:qs+1], op=ALU.mult))
                    vector.wait_ge(t.sem, t.val)
                    for h in (14, 15):
                        ii = vector.tensor_scalar(out=loc[:, qs % 2, h*64:(h+1)*64],
                            in0=PVL[:, _pvoff(h):_pvoff(h)+64],
                            scalar1=aL2[:, h, qs:qs+1], scalar2=None, op0=ALU.mult)
                    resL['b2'] = idve(ii)
                ve(_)
                def _(vector, resL=resL):
                    wait(vector, *[pv_toks[h]['w'] for h in range(14)])
                    vector.reciprocal(out=aL2[:, 0:7, qs:qs+1], in_=pv_den[0])
                    t = idve(vector.reciprocal(out=aL2[:, 7:14, qs:qs+1], in_=pv_den[1]))
                    vector.wait_ge(t.sem, t.val)
                    t = idve(vector.tensor_tensor(out=aL2[:, 0:14, qs:qs+1], in0=aL2[:, 0:14, qs:qs+1],
                                                  in1=sig2[:, 0:14, qs:qs+1], op=ALU.mult))
                    vector.wait_ge(t.sem, t.val)
                    for h in range(14):
                        ii = vector.tensor_scalar(out=loc[:, qs % 2, h*64:(h+1)*64],
                            in0=PVL[:, _pvoff(h):_pvoff(h)+64],
                            scalar1=aL2[:, h, qs:qs+1], scalar2=None, op0=ALU.mult)
                    resL['ev'] = idve(ii)
                ve(_)
            def emit_rsc(h):
                sc = scnt[0] % 4; scnt[0] += 1
                rb = (h % 2)*64; hp = h // 2
                res = {'bk': sc}
                def _(tensor, h=h, sc=sc, rb=rb, hp=hp, res=res):
                    wait(tensor, sc_hist[sc] if sc < 2 else WB_r[sc-2], t_GQTev[g], t_GKT_ev[g])
                    bk = bank_of(sc)
                    res['w'] = ipe(tensor.matmul(bk[0:nl, 0:128],
                        GKT[rb:rb+64, hp, 0:nl],
                        GQT[rb:rb+64, hp, qs*128:(qs+1)*128],
                        start=True, stop=True, tile_position=(rb, 0)))
                pe(_)
                return res
            def emit_rexp(h):
                sc = rsc_toks[h]['bk']
                res = {}
                def _(scalar, h=h, sc=sc, res=res):
                    wait(scalar, rsc_toks[h]['w'], prl_hist[h])
                    bk = bank_of(sc)
                    res['e'] = iact(scalar.activation(out=PRL[0:nl, h, :], in_=bk[0:nl, 0:128], func=AF.Exp))
                    if sc < 2: sc_hist[sc] = res['e']
                    else: WB_r[sc-2] = res['e']
                ac(_)
                return res
            def emit_rpv(h):
                res = {}
                def _(tensor, h=h, res=res):
                    wait(tensor, rexp_toks[h]['e'], t_GKGV[g], t_vones,
                         S['b2'] if h < 7 else S['ev'])
                    off = 1024 + 65*h if h < 7 else 0
                    res['w'] = ipe(tensor.matmul(PVL[:, off:off+65],
                        PRL[0:nl, h, :], GVA[0:nl, h, :], start=True, stop=True))
                    prl_hist[h] = res['w']
                pe(_)
                return res
            def stD():
                for h in range(LH):
                    rsc_toks[h] = emit_rsc(h)
                    rexp_toks[h] = emit_rexp(h)
                    if h >= 2:
                        rpv_toks[h-2] = emit_rpv(h-2)
                rpv_toks[LH-2] = emit_rpv(LH-2)
                rpv_toks[LH-1] = emit_rpv(LH-1)
                t_rscdone[gq] = rsc_toks[LH-1]['w']
            def stE():
                def _(vector):
                    wait(vector, *[rpv_toks[h]['w'] for h in range(LH)])
                    vector.reciprocal(out=aR2[:, 0:7, qs:qs+1],
                                      in_=_pvv[:, 2, 0:455].rearrange("p (h c) -> p h c", h=7)[:, :, 64:65])
                    t = idve(vector.reciprocal(out=aR2[:, 7:8, qs:qs+1], in_=pv_den[0][:, 0:1, :]))
                    vector.wait_ge(t.sem, t.val)
                    for h in range(LH):
                        off = 1024 + 65*h if h < 7 else 0
                        ii = vector.tensor_scalar(out=remp[:, h*64:(h+1)*64],
                            in0=PVL[:, off:off+64],
                            scalar1=aR2[:, h, qs:qs+1], scalar2=None, op0=ALU.mult)
                    S['rp'] = idve(ii)
                ve(_)
                t_PVLfree[0] = S['rp']
                t_combB_prev[0] = S['ev']
            def stF():
                w2_, r2_ = transpose_tiles([remp[:, i*128:(i+1)*128] for i in range(4)],
                                           rempT[:], pre_wait=S['rp'])
                tr = {}
                def _(tensor, r2_=r2_, tr=tr):
                    wait(tensor, r2_, WB_r[0], WB_r[1], t_wA)
                    for nh in range(2):
                        last = None
                        for kc in range(4):
                            last = tensor.matmul(WB[nh][:], rempT[:, kc, :], rout[:, kc, nh*512:(nh+1)*512],
                                                 start=(kc == 0), stop=(kc == 3))
                        tr[('ro', nh)] = ipe(last)
                        WB_w[nh] = tr[('ro', nh)]
                pe(_)
                def _(vector, tr=tr, tad=t_attn_done[0]):
                    wait(vector, tr[('ro', 0)], tr[('ro', 1)], S['ev'], t_sigv[g], tad)
                    for h in range(NH):
                        ii = vector.scalar_tensor_tensor(out=attn[:, qs % 2, h*64:(h+1)*64],
                            in0=WB[h // 8][:, (h % 8)*64:(h % 8)*64+64],
                            scalar=sig2[:, 16+h, qs:qs+1],
                            in1=loc[:, qs % 2, h*64:(h+1)*64], op0=ALU.mult, op1=ALU.add)
                    tkk = idve(ii)
                    WB_r[0] = WB_r[1] = tkk
                    tr['comb'] = tkk
                ve(_)
                w3_, r3_ = transpose_tiles([attn[:, qs % 2, dc*128:(dc+1)*128] for dc in range(8)],
                                           attnT[:, :, (qs % 2)*128:(qs % 2 + 1)*128],
                                           pre_wait=tr['comb'])
                t_attn_done[0] = w3_
                resW = {}
                def _(tensor, r3_=r3_, resW=resW):
                    wait(tensor, r3_, WB_r[0], WB_r[1], t_wA)
                    for nh in range(2):
                        last = None
                        for kc in range(8):
                            last = tensor.matmul(WB[nh][:], attnT[:, kc, (qs % 2)*128:(qs % 2 + 1)*128],
                                                 wout[:, kc, nh*512:(nh+1)*512], start=(kc == 0), stop=(kc == 7))
                        resW[nh] = ipe(last)
                        WB_w[nh] = resW[nh]
                pe(_)
                def _(vector, resW=resW):
                    wait(vector, resW[0], resW[1], t_x2st[gq-2] if gq >= 2 else None)
                    vector.tensor_tensor(out=x2buf[:, gq % 2, 0:512], in0=WB[0][:], in1=xbuf[:, gq % 4, 0:512], op=ALU.add)
                    tkk = idve(vector.tensor_tensor(out=x2buf[:, gq % 2, 512:1024], in0=WB[1][:], in1=xbuf[:, gq % 4, 512:1024], op=ALU.add))
                    WB_r[0] = WB_r[1] = tkk
                    resW['ev'] = tkk
                    t_xfree[gq] = tkk
                ve(_)
                def _(sync, resW=resW):
                    wait(sync, resW['ev'])
                    t_x2st[gq] = dinc(s_x2w[gq % 2])(sync.dma_start(out=x2_d[gq*128:(gq+1)*128, :], in_=x2buf[:, gq % 2, :]))
                    nq2 = gq + 4
                    if nq2 < NQ:
                        wait(sync, t_xfree[nq2 - 4])
                        t_x[nq2] = dinc(s_xld[nq2 % 4])(sync.dma_start(out=xbuf[:, nq2 % 4, :], in_=x_d[nq2*128:(nq2+1)*128, :]))
                sy(_)
            return stA, stB, stC, stD, stE, stF

        st = [qsub_stages(g, qs) for qs in range(4)]
        st[0][0](); st[0][1](); st[0][2](); st[0][3](); st[0][4]()
        for qs in range(1, 4):
            st[qs][0](); st[qs][1]()
            st[qs-1][5]()
            st[qs][2](); st[qs][3](); st[qs][4]()
        st[3][5]()

    t_peA = Tok(s_pe, cnt.get("s_pe", 0))
    t_dveA = Tok(s_dve, cnt.get("s_dve", 0))
    t_actA = Tok(s_act, cnt.get("s_act", 0))
    t_poolA = Tok(s_pool, cnt.get("s_pool", 0))
    t_stA = Tok(s_gv, cnt.get("s_gv", 0))
    ctxA.close()

    B_w = [None]*6; B_r = [None]*6
    ctxB = ExitStack()
    wit = sb(ctxB,"wit_s",[128,8,F]); wot = sb(ctxB,"wot_s",[128,32,D])
    glA = sb(ctxB,"glA_s",[128,8,40]); lorB = sb(ctxB,"lorB_s",[32,F])
    wib = sb(ctxB,"wib_s",[128,32],F32); bfull = sb(ctxB,"bfull_s",[128,D],F32)
    glbrow = sb(ctxB,"glbrow_s",[1,40])
    interT = sb(ctxB,"interT",[128,4,256])
    ln2 = sb(ctxB,"ln2",[128,4,D]); ln2T = sb(ctxB,"ln2T",[128,24,256])
    mtb = sb(ctxB,"mtb",[128,4,32]); mtT = sb(ctxB,"mtT",[32,512])
    glt = stage[:, 1024:1184].rearrange("p (a b) -> p a b", a=4)
    ohb = stage[:, 1184:1216].rearrange("p (a b) -> p a b", a=4)
    mxb = stage[:, 1216:1220]
    x2ld = sb(ctxB,"x2ld",[128,6,D],F32)

    proB = {}
    def _(sync):
        wait(sync, t_peA, t_dveA, t_actA, t_poolA, t_stA, t_x2st[NQ-1], t_x2st[NQ-2])
        proB['wib'] = dinc(s_wib)(sync.dma_start(out=wib[:], in_=din["wib"]))
        proB['b'] = dinc(s_wib)(sync.dma_start(out=bfull[:], in_=din["wob"].broadcast_to([128, D])))

    sy(_)
    def _(vector):
        wait(vector, t_peA, t_actA, t_poolA, t_stA)
        vector.engine_nop()
    ve(_)
    t_wBs = cast_weights(wplan([("wit",wit,None),("wot",wot,None),("glA",glA,None),("lorB",lorB,None),("glb",glbrow,None)]), stage)

    NGB = S // 256
    t_outst = [None]*NQ
    t_ln2w = [None]*(NGB+4)
    t_lnT2ev = [None]*(NGB+4)
    t_mtev = [None]*(NGB+4)
    t_gate = [None]*(NGB+4)
    t_bias = [None]*(NGB+4)
    resG_mt = [None]*(NGB+4)
    t_mtw = [None]*(NGB+4)

    t_lnap = [[None, None] for _ in range(NGB+4)]

    def pre_ln_qs(g, qs):
        gq = 2*g + qs
        par = g % 2
        tkx = {}
        def _(sync, gq=gq, tkx=tkx):
            wait(sync, t_x2st[gq], t_outst[gq-6] if gq >= 6 else None)
            tkx['t'] = dinc(s_x2r[gq % 2])(sync.dma_start(out=x2ld[:, gq % 6, :], in_=x2_d[gq*128:(gq+1)*128, :]))
        sy(_)
        ln_toks = layer_norm(1, [x2ld[:, gq % 6, :]], [tkx['t']],
                             [ln2[:, par*2+qs, :]], stage[:, 0:1024],
                             pre=(t_ln2w[g-2] if g >= 2 else None, t_wBs))
        t_lnap[g][qs] = ln_toks[0]
        def _(gpsimd, gq=gq, g=g, qs=qs, lt=ln_toks[0]):
            wait(gpsimd, lt, proB['b'])
            tk = ipool(gpsimd.tensor_tensor(out=x2ld[:, gq % 6, :], in0=x2ld[:, gq % 6, :], in1=bfull[:], op=ALU.add))
            if qs == 1: t_bias[g] = tk
        po(_)

    def pre_pe(g):
        par = g % 2
        r3 = g % 3
        for qs in range(2):
            w_, r_ = transpose_tiles([ln2[:, par*2+qs, dc*128:(dc+1)*128] for dc in range(8)],
                                     ln2T[:, r3*8:r3*8+8, qs*128:(qs+1)*128],
                                     pre_wait=t_lnap[g][qs])
            if qs == 1:
                t_ln2w[g] = w_
                t_lnT2ev[g] = r_
        resG = {}
        def _(tensor, g=g, r3=r3, resG=resG):
            wait(tensor, t_lnT2ev[g], t_gate[g-1] if g >= 1 else None, t_wBs, t_const)
            last = None
            for qs in range(2):
                for kc in range(8):
                    tensor.matmul(PVL[:, 1024+qs*128:1024+qs*128+40], ln2T[:, r3*8+kc, qs*128:(qs+1)*128],
                                  glA[:, kc, :], start=(kc == 0), stop=False)
                last = tensor.matmul(PVL[:, 1024+qs*128:1024+qs*128+40], onesb[:], glbrow[:], start=False, stop=True)
            resG['w'] = ipe(last)
        pe(_)
        def _(vector, g=g, par=par, resG=resG):
            wait(vector, resG['w'])
            for qs in range(2):
                ii = vector.tensor_copy(out=glt[:, par*2+qs, :], in_=PVL[:, 1024+qs*128:1024+qs*128+40])
            t_gate[g] = idve(ii)
            resG['ev'] = t_gate[g]
        ve(_)
        def _(vector, g=g, par=par, resG=resG):
            wait(vector, resG['ev'], t_mtw[g-2] if g >= 2 else None)
            for qs in range(2):
                ii = vector.reduce_max(out=mxb[:, par*2+qs:par*2+qs+1], in_=glt[:, par*2+qs, 0:8], axis=AX.X)
            t = idve(ii); vector.wait_ge(t.sem, t.val)
            for qs in range(2):
                ii = vector.tensor_scalar(out=ohb[:, par*2+qs, :], in0=glt[:, par*2+qs, 0:8], scalar1=mxb[:, par*2+qs:par*2+qs+1], scalar2=None, op0=ALU.is_equal)
            t = idve(ii); vector.wait_ge(t.sem, t.val)
            for qs in range(2):
                ii = vector.tensor_tensor(out=mtb[:, par*2+qs, :].rearrange("p (a b) -> p a b", a=8),
                                     in0=glt[:, par*2+qs, 8:40].rearrange("p (a b) -> p a b", a=8),
                                     in1=ohb[:, par*2+qs, :].unsqueeze(2).broadcast_to([128, 8, 4]), op=ALU.mult)
            resG['mt'] = idve(ii)
        ve(_)
        resG_mt[g] = resG['mt']

    def mtb_t(g):
        par = g % 2
        w_, ev_mt = transpose_tiles([mtb[:, par*2, :], mtb[:, par*2+1, :]], mtT[:, par*256:par*256+256], pre_wait=resG_mt[g], rows=32)
        t_mtev[g] = ev_mt
        t_mtw[g] = w_

    def body(g):
        gq0 = 2*g
        par = g % 2
        relu_toks = [None]*32
        wo_toks = [None]*32
        def wo_out(k):
            return WB[k][:, 0:512] if k < 2 else PVL[:, (k-2)*512:(k-1)*512]
        resW = {}
        def emit_wo(mtn):
            def _(tensor, mtn=mtn):
                need = [relu_toks[mtn], t_wBs]
                if mtn == 0:
                    need += [B_r[2], B_r[3], B_r[4], B_r[5]]
                wait(tensor, *need)
                tkk = None
                for qs in range(2):
                    for nh in range(2):
                        tkk = tensor.matmul(wo_out(qs*2 + nh), interT[:, mtn % 4, qs*128:(qs+1)*128],
                                            wot[:, mtn, nh*512:(nh+1)*512], start=(mtn == 0), stop=(mtn == 31))
                wo_toks[mtn] = ipe(tkk)
                if mtn == 31:
                    resW['wo'] = wo_toks[31]
                    for i in (2,3,4,5): B_w[i] = wo_toks[31]
            pe(_)
        for mtn in range(32):
            bb = mtn % 2
            resB = {}
            def _(tensor, mtn=mtn, bb=bb, resB=resB, g=g, par=par):
                wait(tensor, B_r[bb], t_lnT2ev[g], t_mtev[g], t_wBs)
                r3 = g % 3
                last = None
                for kc in range(8):
                    last = tensor.matmul(SC[bb][:, 0:256], wit[:, kc, mtn*128:(mtn+1)*128], ln2T[:, r3*8+kc, :],
                                         start=(kc == 0), stop=False)
                last = tensor.matmul(SC[bb][:, 0:256], lorB[:, mtn*128:(mtn+1)*128], mtT[:, par*256:par*256+256], start=False, stop=True)
                resB['w'] = ipe(last); B_w[bb] = resB['w']
            pe(_)
            def _(vector, mtn=mtn, bb=bb, resB=resB, relu_toks=relu_toks, wo_toks=wo_toks):
                wait(vector, resB['w'], proB['wib'], wo_toks[mtn-4] if mtn >= 4 else None)
                B_r[bb] = idve(vector.tensor_scalar(out=interT[:, mtn % 4, :], in0=SC[bb][:, 0:256],
                                                    scalar1=wib[:, mtn:mtn+1], scalar2=0.0,
                                                    op0=ALU.add, op1=ALU.max))
                relu_toks[mtn] = B_r[bb]
            ve(_)
            if mtn >= 2:
                emit_wo(mtn - 2)
        emit_wo(30)
        emit_wo(31)
        def _(vector, resW=resW, g=g, gq0=gq0):
            wait(vector, resW['wo'], t_bias[g])
            for qs in range(2):
                for nh in range(2):
                    ii = vector.tensor_tensor(out=x2ld[:, (gq0+qs) % 6, nh*512:(nh+1)*512], in0=wo_out(qs*2 + nh),
                                         in1=x2ld[:, (gq0+qs) % 6, nh*512:(nh+1)*512], op=ALU.add)
            tkk = idve(ii)
            for i in (2,3,4,5): B_r[i] = tkk
            resW['ev'] = tkk
        ve(_)
        def _(sync, gq0=gq0, resW=resW):
            wait(sync, resW['ev'])
            for qs in range(2):
                gq = gq0 + qs
                t_outst[gq] = dinc(s_outw[gq % 2])(sync.dma_start(out=out_d[gq*128:(gq+1)*128, :], in_=x2ld[:, gq % 6, :]))
        sy(_)

    for g in (0, 1):
        pre_ln_qs(g, 0)
        pre_ln_qs(g, 1)
        pre_pe(g)
    for g in range(NGB):
        mtb_t(g)
        if g + 2 < NGB:
            pre_ln_qs(g + 2, 0)
            pre_ln_qs(g + 2, 1)
        body(g)
        if g + 2 < NGB:
            pre_pe(g + 2)

    def _(gpsimd):
        wait(gpsimd, *[t_outst[gq] for gq in range(NQ)])
    po(_)
    ctxB.close()
    ctx.close()
    _group_blocks_by_engine(nc)
    return nc

def _group_blocks_by_engine(nc):
    pats = ("_SP_", "_Activation_", "_PE_", "_DVE_", "_Pool_")
    f = list(nc.m.functions)[0]
    blocks = list(f.blocks)
    if len(blocks) < 3:
        return
    mid = blocks[1:-1]
    def key(b):
        for j, p in enumerate(pats):
            if p in b.name:
                return j
        return None
    keys = [key(b) for b in mid]
    if any(k is None for k in keys):
        return
    f.blocks = [blocks[0]] + [b for _, b in sorted(zip(keys, mid), key=lambda t: t[0])] + [blocks[-1]]

def _prep(inputs):
    f32 = np.float32
    g = lambda k: np.ascontiguousarray(np.asarray(inputs[k], f32))
    w = {}
    w["wq"] = g("L0_attn_wq"); w["wk"] = g("L0_attn_wk"); w["wv"] = g("L0_attn_wv")
    w["wgq"] = g("L0_attn_wgq"); w["wgk"] = g("L0_attn_wgk"); w["wgv"] = g("L0_attn_wgv")
    w["rout"] = g("L0_attn_r_out_w"); w["wout"] = g("L0_attn_w_out")
    glw = g("L0_attn_gate_local_w"); grw = g("L0_attn_gate_remote_w")
    gb = np.zeros((D, 32), f32)
    for h in range(NH):
        gb[h*64:(h+1)*64, h] = glw[h]
        gb[h*64:(h+1)*64, 16+h] = grw[h]
    w["gblk"] = gb
    w["wit"] = np.ascontiguousarray(g("L0_ffn_wi_W").T)
    w["wot"] = np.ascontiguousarray(g("L0_ffn_wo_W").T)
    gA = np.concatenate([g("L0_ffn_gate_W").T, g("L0_ffn_lora_A").reshape(32, D).T], axis=1)
    w["glA"] = np.ascontiguousarray(gA)
    glb = np.zeros((1, 40), f32); glb[0, 0:8] = g("L0_ffn_gate_b")
    w["glb"] = glb
    w["lorB"] = np.ascontiguousarray(g("L0_ffn_lora_B").transpose(0, 2, 1).reshape(32, F))
    w["wib"] = np.ascontiguousarray(g("L0_ffn_wi_b").reshape(32, 128).T)
    w["wob"] = g("L0_ffn_wo_b").reshape(1, D)
    bf = ml_dtypes.bfloat16
    w["identc"] = np.eye(128, dtype=bf)
    kk = np.arange(128)[:, None]; qq = np.arange(128)[None, :]
    M0 = np.where(kk > qq, 1.0, 0.0).astype(np.float16)
    M2 = np.where(kk <= qq, 1.0, 0.0).astype(np.float16)
    w["maskc"] = np.ascontiguousarray(np.concatenate([M0, np.ones((128, 128), np.float16), M2], axis=1))
    w["onesb"] = np.ones((1, 128), bf)
    return w

def kernel(**inputs):
    x = np.ascontiguousarray(np.asarray(inputs["L0_attn_x"], np.float32))
    Bn, S, _ = x.shape
    w = _prep(inputs)
    nc = build_program(S=S)
    in_maps = []
    for b in range(Bn):
        m = dict(w); m["x"] = x[b]
        in_maps.append(m)
    res = run_bass_kernel_spmd(nc, in_maps, core_ids=list(range(Bn)))
    return np.stack([np.asarray(r["out"], np.float32) for r in res.results], axis=0)
